# version 19
# baseline (speedup 1.0000x reference)
import contextlib
import numpy as np
import ml_dtypes
import concourse.bass as bass
import concourse.mybir as mybir
from concourse.bass_utils import run_bass_kernel_spmd

F32 = mybir.dt.float32
BF16 = mybir.dt.bfloat16
AF = mybir.ActivationFunctionType
ALU = mybir.AluOpType
AX = mybir.AxisListType

D = 1024
DIN = 2048
NH = 32
HD = 64
QC = 128
NZC = 16
NXC = 18
EPS = 1e-5
UN = 1024


def _swz(w):
    O, I = w.shape
    return np.ascontiguousarray(
        w.reshape(O // 128, 128, I // 128, 128).transpose(0, 3, 2, 1))


def _cols(s):
    return np.ascontiguousarray(np.asarray(s).reshape(-1, 128).T)


def host_prepare(inp, TP, TM):
    f = lambda a: np.ascontiguousarray(np.asarray(a, dtype=np.float32))
    sh = {}
    sh["wip"] = _swz(f(inp["L0_attn_in_proj_w"])[:4352])
    wdt = f(inp["L0_attn_in_proj_w"])[4352:4384]
    sh["wipdt"] = np.ascontiguousarray(wdt.reshape(32, 8, 128).transpose(2, 1, 0))
    sh["wop"] = _swz(f(inp["L0_attn_out_proj_w"]))
    sh["w1"] = _swz(f(inp["L0_ffn_w1_w"]))
    sh["w2"] = _swz(f(inp["L0_ffn_w2_w"]))
    sh["h1"] = _swz(f(inp["L0_ffn_h1_w"]))
    sh["h2"] = _swz(f(inp["L0_ffn_h2_w"]))
    sh["w3"] = _swz(f(inp["L0_ffn_w3_w"]))
    sh["w1s"] = _cols(f(inp["L0_ffn_w1_s"]))
    sh["w2s"] = _cols(f(inp["L0_ffn_w2_s"]))
    sh["h1s"] = _cols(f(inp["L0_ffn_h1_s"]))
    sh["h2s"] = _cols(f(inp["L0_ffn_h2_s"]))
    sh["w3s"] = _cols(f(inp["L0_ffn_w3_s"]))
    sh["normw"] = _cols(f(inp["L0_attn_norm_w"]))
    sh["dtb"] = f(inp["L0_attn_dt_bias"]).reshape(32, 1)
    sh["alog"] = f(inp["L0_attn_A_log"]).reshape(32, 1)
    sh["dexp"] = _cols(np.repeat(f(inp["L0_attn_D"]), HD))
    sh["convb"] = _cols(f(inp["L0_attn_conv_b"]))
    cw = f(inp["L0_attn_conv_w"])
    cd = np.zeros((NXC, 128, 4, 128), np.float32)
    idx = np.arange(128)
    for cc in range(NXC):
        for k in range(4):
            cd[cc, idx, k, idx] = cw[cc * 128 + idx, k]
    sh["convdiag"] = cd
    sh["lnw_b"] = np.ascontiguousarray(
        np.broadcast_to(f(inp["L0_attn_ln_w"])[None, :], (128, D)))
    sh["lnb_b"] = np.ascontiguousarray(
        np.broadcast_to(f(inp["L0_attn_ln_b"])[None, :], (128, D)))
    sh["identf"] = np.eye(128, dtype=np.float32)
    sh["id32"] = np.eye(32, dtype=np.float32)
    sh["ones1"] = np.ones((1, 128), np.float32)
    sh["ones128"] = np.ones((128, 1), np.float32)
    jj, ii = np.meshgrid(np.arange(128), np.arange(128), indexing="ij")
    sh["cmt"] = (ii >= jj).astype(np.float32)
    cm = np.ones((32, max(TP, TM)), np.float32)
    cm[:, ::QC] = 0.0
    sh["cmask"] = cm

    x = f(inp["L0_attn_x"])
    maps = []
    for bb in range(x.shape[0]):
        for hf in range(2):
            m = dict(sh)
            if hf == 0:
                xin = np.concatenate(
                    [np.zeros((TP, D), np.float32), x[bb, :TM]], axis=0)
                pm = np.zeros((128, TP), np.float32)
            else:
                xin = x[bb, :TP + TM]
                pm = np.ones((128, TP), np.float32)
            m["x_in"] = np.ascontiguousarray(xin)
            m["pmask"] = pm
            maps.append(m)
    return maps


class Bld:
    def __init__(self, nc):
        self.nc = nc
        self.plan = {k: [] for k in ("sync", "gpsimd", "tensor", "vector", "scalar")}
        self.sem = {}
        self.cnt = {}
        self.stack = contextlib.ExitStack()
        self._cms = {}
        self._order = []

    def new_sem(self, name):
        self.sem[name] = self.stack.enter_context(self.nc.semaphore(name))
        self.cnt[name] = 0

    def emit(self, eng, fn, incs=()):
        incs2 = [(self.sem[s], n) for s, n in incs]
        for s, n in incs:
            self.cnt[s] += n

        def run(e, fn=fn, incs2=incs2):
            ins = fn(e)
            for h, n in incs2:
                ins = ins.then_inc(h, n)
        self.plan[eng].append(run)

    def wait(self, eng, s, val):
        h = self.sem[s]
        self.plan[eng].append(lambda e, h=h, v=val: e.wait_ge(h, v))

    def barrier(self):
        esem = {"tensor": "eT", "vector": "eV", "scalar": "eS",
                "sync": "qS", "gpsimd": "qG"}
        engs = list(esem)
        for e in ("tensor", "vector", "scalar"):
            self.emit(e, lambda x: x.drain(), [(esem[e], 1)])
        for e in ("sync", "gpsimd"):
            self.emit(e, lambda x: x.nop(), [(esem[e], 1)])
        marks = {e: self.cnt[esem[e]] for e in engs}
        for e in engs:
            if e == "gpsimd":
                continue
            for o in engs:
                if o != e:
                    self.wait(e, esem[o], marks[o])

    def sbuf(self, name, shape, dtype):
        cm = self.nc.sbuf_tensor(name, shape, dtype)
        h = cm.__enter__()
        self._cms[name] = cm
        self._order.append(name)
        return h

    def psum(self, name, shape, dtype=F32):
        cm = self.nc.psum_tensor(name, shape, dtype)
        h = cm.__enter__()
        self._cms[name] = cm
        self._order.append(name)
        return h

    def free(self, *names):
        for n in sorted(names, key=lambda n: -self._order.index(n)):
            self._cms.pop(n).__exit__(None, None, None)

    def dump_guard(self):
        for eng in ("vector", "scalar", "tensor"):
            self.wait(eng, "dO", self.cnt["dO"])


class Arena:

    def __init__(self, bld, n_units):
        self.t = bld.sbuf("ARENA", [128, n_units * UN], BF16)
        self.free_units = [(0, n_units)]
        self.alloc_map = {}

    def _take(self, n):
        for i, (st, ln) in enumerate(self.free_units):
            if ln >= n:
                if ln == n:
                    self.free_units.pop(i)
                else:
                    self.free_units[i] = (st + n, ln - n)
                return st
        raise RuntimeError(f"arena OOM ({n} units); free={self.free_units}")

    def alloc(self, name, n, blk=None):
        blk = blk or n
        assert n % blk == 0
        starts = [self._take(blk) for _ in range(n // blk)]
        self.alloc_map[name] = (blk, starts)

    def free(self, *names):
        for name in names:
            blk, starts = self.alloc_map.pop(name)
            for st in starts:
                self.free_units.append((st, blk))
        self.free_units.sort()
        merged = []
        for st, ln in self.free_units:
            if merged and merged[-1][0] + merged[-1][1] == st:
                merged[-1] = (merged[-1][0], merged[-1][1] + ln)
            else:
                merged.append((st, ln))
        self.free_units = merged

    def sl(self, name, col0, w, f32=False):
        blk, starts = self.alloc_map[name]
        bw = blk * UN
        c0 = 2 * col0 if f32 else col0
        ww = 2 * w if f32 else w
        bi, off = divmod(c0, bw)
        assert off + ww <= bw, (name, col0, w)
        base = starts[bi] * UN + off
        ap = self.t[:, base: base + ww]
        return ap.bitcast(F32) if f32 else ap

    def nblk(self, name):
        blk, starts = self.alloc_map[name]
        return len(starts), blk * UN

    def frz(self, name, f32=False):
        blk, starts = self.alloc_map[name]
        t = self.t

        def sl(col0, w, _blk=blk, _st=tuple(starts), _f32=f32, _t=t):
            bw = _blk * UN
            c0 = 2 * col0 if _f32 else col0
            ww = 2 * w if _f32 else w
            bi, off = divmod(c0, bw)
            assert off + ww <= bw
            base = _st[bi] * UN + off
            ap = _t[:, base: base + ww]
            return ap.bitcast(F32) if _f32 else ap
        return sl


def build_nc(TP, TM, debug=False, rc=False):
    assert TP == TM
    TH = TM // 2
    U_T = max(TM // UN, 1)
    nc = bass.Bass("TRN2", target_bir_lowering=False, detect_race_conditions=rc)
    dram = {}

    def din(name, shape):
        dram[name] = nc.dram_tensor(name, list(shape), F32, kind="ExternalInput").ap()
        return dram[name]

    x_in = din("x_in", (TP + TM, D))
    din("pmask", (128, TP))
    din("cmask", (32, max(TP, TM)))
    wip = din("wip", (34, 128, 8, 128))
    wipdt = din("wipdt", (128, 8, 32))
    wop = din("wop", (8, 128, 16, 128))
    w1 = din("w1", (32, 128, 8, 128)); w2 = din("w2", (32, 128, 8, 128))
    h1 = din("h1", (32, 128, 32, 128)); h2 = din("h2", (32, 128, 32, 128))
    w3 = din("w3", (8, 128, 32, 128))
    for nm, shp in [("w1s", (128, 32)), ("w2s", (128, 32)), ("h1s", (128, 32)),
                    ("h2s", (128, 32)), ("w3s", (128, 8)), ("normw", (128, 16)),
                    ("dtb", (32, 1)), ("alog", (32, 1)), ("dexp", (128, 16)),
                    ("convb", (128, 18)), ("convdiag", (NXC, 128, 4, 128)),
                    ("lnw_b", (128, D)), ("lnb_b", (128, D)),
                    ("identf", (128, 128)), ("id32", (32, 32)),
                    ("ones1", (1, 128)), ("ones128", (128, 1)),
                    ("cmt", (128, 128))]:
        din(nm, shp)

    y_out = nc.dram_tensor("y_out", [TM, D], F32, kind="ExternalOutput").ap()
    xn_scr = nc.dram_tensor("xn_scr", [128, 8 * TM], BF16).ap()
    zs_scr = nc.dram_tensor("zs_scr", [128, NZC * TM], BF16).ap()

    dbg = {}
    if debug:
        def dout(name, shape, dtype):
            dbg[name] = nc.dram_tensor(name, list(shape), dtype,
                                       kind="ExternalOutput").ap()
        dout("d_xn2m", (128, 8 * TM), BF16)
        dout("d_zs0", (128, NZC * TM), BF16)
        dout("d_xcvx", (128, 16 * TM), BF16)
        dout("d_bcv", (128, TM), BF16)
        dout("d_ccv", (128, TM), BF16)
        dout("d_dt", (32, TM), F32)
        dout("d_sloc", (32, TM), F32)
        dout("d_H0", (128, 2048), F32)
        dout("d_xbct", (128, NXC * 3), BF16)
        dout("d_xbct2", (128, NXC * 3), BF16)

        dout("d_yg", (128, NZC * TM), BF16)
        dout("d_ynorm", (128, NZC * TM), BF16)
        dout("d_acc", (128, 8 * TM), BF16)
        dout("d_g", (128, 32 * TH), BF16)
        dout("d_hh", (128, 32 * TH), BF16)
        dout("d_xbf", (128, D), BF16)
        dout("d_xbf2", (128, D), BF16)
        dout("d_cols", (128, 8), F32)
        dout("d_xnb", (128, D), F32)
        dout("d_lnw", (128, D), F32)
        dout("d_lnb", (128, D), F32)
        dout("d_xt", (128, D), BF16)
        dout("d_idb", (128, 128), BF16)
        dout("d_pm", (128, TP), BF16)

    b = Bld(nc)
    for s in ("dS", "dG", "dO", "eT", "eV", "eS", "qS", "qG",
              "dG0", "dG1", "dX0", "dX1", "dZ0", "dZ1", "dF0", "dF1",
              "dO0", "dO1"):
        b.new_sem(s)

    identf_s = b.sbuf("identf_s", [128, 128], F32)
    identb_s = b.sbuf("identb_s", [128, 128], BF16)
    id32b_s = b.sbuf("id32b_s", [32, 32], BF16)
    ones1b_s = b.sbuf("ones1b_s", [1, 128], BF16)
    ones128b_s = b.sbuf("ones128b_s", [128, 1], BF16)
    cmtb_s = b.sbuf("cmtb_s", [128, 128], BF16)
    dexp_s = b.sbuf("dexp_s", [128, 16], F32)
    normw_s = b.sbuf("normw_s", [128, 16], F32)
    w1s_s = b.sbuf("w1s_s", [128, 32], F32); w2s_s = b.sbuf("w2s_s", [128, 32], F32)
    h1s_s = b.sbuf("h1s_s", [128, 32], F32); h2s_s = b.sbuf("h2s_s", [128, 32], F32)
    w3s_s = b.sbuf("w3s_s", [128, 8], F32)
    dtb_s = b.sbuf("dtb_s", [32, 1], F32)
    dtbm30_s = b.sbuf("dtbm30_s", [32, 1], F32)
    alog_s = b.sbuf("alog_s", [32, 1], F32)
    acol_s = b.sbuf("acol_s", [32, 1], F32)
    convb_s = b.sbuf("convb_s", [128, 18], F32)
    cmask_s = b.sbuf("cmask_s", [64, max(TP, TM)], BF16)
    xbct_s = b.sbuf("xbct_s", [128, NXC * 3], BF16)
    pmask_s = b.sbuf("pmask_s", [128, TP], BF16)
    WBUF = [b.sbuf("wbuf0", [128, 4096], BF16), b.sbuf("wbuf1", [128, 4096], BF16)]
    CDBUF = [b.sbuf("cdbuf0", [128, 512], BF16), b.sbuf("cdbuf1", [128, 512], BF16)]

    A = Arena(b, max(40 * U_T, 48))
    A.alloc("H", 4)
    H_s = A.sl("H", 0, 2048, f32=True)
    A.alloc("lnw", 2); A.alloc("lnb", 2)
    lnw_ap = A.sl("lnw", 0, D, f32=True)
    lnb_ap = A.sl("lnb", 0, D, f32=True)
    lnw = lambda _a=lnw_ap: _a
    lnb = lambda _a=lnb_ap: _a

    b.emit("gpsimd", lambda e: e.dma_start(cmask_s[32:64, :], dram["cmask"][:]),
           [("dG", 16)])
    for name, dst in [("identf", identf_s),
                      ("dexp", dexp_s), ("normw", normw_s), ("w1s", w1s_s),
                      ("w2s", w2s_s), ("h1s", h1s_s), ("h2s", h2s_s),
                      ("w3s", w3s_s), ("dtb", dtb_s), ("alog", alog_s),
                      ("convb", convb_s)]:
        b.emit("sync", lambda e, d=dst, s=dram[name]: e.dma_start(d[:], s[:]),
               [("dS", 16)])
    b.emit("sync", lambda e: e.dma_start(lnw(), dram["lnw_b"][:]), [("dS", 16)])
    b.emit("sync", lambda e: e.dma_start(lnb(), dram["lnb_b"][:]), [("dS", 16)])
    for name, dst in [("identf", identb_s), ("id32", id32b_s), ("ones1", ones1b_s),
                      ("ones128", ones128b_s), ("cmt", cmtb_s)]:
        b.emit("gpsimd", lambda e, d=dst, s=dram[name]: e.dma_start(d[:], s[:]),
               [("dG", 16)])
    b.emit("gpsimd", lambda e: e.dma_start(pmask_s[:], dram["pmask"][:]),
           [("dG", 16)])
    for eng in ("vector", "scalar", "tensor"):
        b.wait(eng, "dS", b.cnt["dS"])
        b.wait(eng, "dG", b.cnt["dG"])
    b.emit("scalar", lambda e: e.activation(acol_s[:], alog_s[:], AF.Exp), [("eS", 1)])
    b.wait("vector", "eS", b.cnt["eS"])
    b.emit("vector", lambda e: e.tensor_scalar_mul(acol_s[:], acol_s[:], -1.0))
    b.emit("vector", lambda e: e.tensor_scalar_add(dtbm30_s[:], dtb_s[:], -30.0))
    b.emit("vector", lambda e: e.memset(H_s, 0.0))
    b.emit("vector", lambda e: e.memset(xbct_s[:], 0.0), [("eV", 1)])

    def ln_phase(Tn, x_off, xn2sl, to_dram, tag):
        NT = Tn // 128
        xsb = [b.sbuf(f"lx{tag}{i}", [128, D], BF16) for i in range(2)]
        xnb = b.sbuf(f"lxn{tag}", [128, D], F32)
        xsq = b.sbuf(f"lsq{tag}", [128, D], BF16)
        xbf = b.sbuf(f"lbf{tag}", [128, D], BF16)
        xbf2 = b.sbuf(f"lbf2{tag}", [128, D], BF16)
        xst = [b.sbuf(f"lst{tag}{i}", [128, D], BF16) for i in range(2)]
        cols = b.sbuf(f"lcol{tag}", [128, 8], F32)
        psA = b.psum(f"lpsA{tag}", [128, 512])
        psB = b.psum(f"lpsB{tag}", [128, 512])
        tile_done = [0] * NT
        stage_done = [0] * NT
        ps_free = {0: b.cnt["eV"], 1: b.cnt["eV"]}

        xsqf = xsq[:].bitcast(F32)

        def vw():
            b.wait("vector", "eV", b.cnt["eV"])

        def stats(src):
            b.emit("vector", lambda e: e.reduce_sum(cols[:, 0:1], src[:], axis=AX.X))
            b.emit("vector", lambda e: e.tensor_tensor(
                xsqf, src[:, 0:512], src[:, 0:512], ALU.mult))
            b.emit("vector", lambda e: e.reduce_sum(cols[:, 6:7], xsqf, axis=AX.X))
            b.emit("vector", lambda e: e.tensor_tensor(
                xsqf, src[:, 512:1024], src[:, 512:1024], ALU.mult))
            b.emit("vector", lambda e: e.reduce_sum(cols[:, 7:8], xsqf, axis=AX.X),
                   [("eV", 1)])
            vw()
            b.emit("vector", lambda e: e.tensor_add(
                cols[:, 1:2], cols[:, 6:7], cols[:, 7:8]))
            b.emit("vector", lambda e: e.tensor_scalar_mul(
                cols[:, 2:3], cols[:, 0:1], 1.0 / D), [("eV", 1)])
            vw()
            b.emit("vector", lambda e: e.tensor_scalar_mul(cols[:, 3:4], cols[:, 1:2], 1.0 / D))
            b.emit("vector", lambda e: e.tensor_tensor(
                cols[:, 4:5], cols[:, 2:3], cols[:, 2:3], ALU.mult), [("eV", 1)])
            vw()
            b.emit("vector", lambda e: e.tensor_tensor(
                cols[:, 4:5], cols[:, 3:4], cols[:, 4:5], ALU.subtract), [("eV", 1)])
            vw()
            b.emit("vector", lambda e: e.tensor_scalar(
                cols[:, 4:5], cols[:, 4:5], 1.0, EPS, ALU.mult, ALU.add), [("eV", 1)])
            vw()
            b.emit("vector", lambda e: e.tensor_scalar(
                cols[:, 5:6], cols[:, 4:5], 1.0 + EPS, -EPS, ALU.mult, ALU.add),
                [("eV", 1)])
            vw()
            b.emit("vector", lambda e: e.reciprocal(cols[:, 6:8], cols[:, 4:6]), [("eV", 1)])
            b.wait("scalar", "eV", b.cnt["eV"])
            b.emit("scalar", lambda e: e.activation(
                cols[:, 6:8], cols[:, 6:8], AF.Sqrt), [("eS", 1)])
            b.wait("vector", "eS", b.cnt["eS"])

        b.wait("gpsimd", "eV", b.cnt["eV"])
        b.wait("gpsimd", "eS", b.cnt["eS"])
        b.wait("gpsimd", "eT", b.cnt["eT"])
        for i in range(NT):
            if i >= 2:
                b.wait("gpsimd", "eV", tile_done[i - 2])
            b.emit("gpsimd", lambda e, i=i: e.dma_start(
                xsb[i % 2][:], x_in[x_off + i * 128: x_off + (i + 1) * 128, :]),
                [(f"dG{i % 2}", 16)])
            ldm = b.cnt[f"dG{i % 2}"]
            xt = xsb[i % 2]
            b.wait("vector", f"dG{i % 2}", ldm)
            stats(xt)
            if to_dram:
                b.emit("vector", lambda e, xt=xt: e.tensor_scalar(
                    xbf2[:], xt[:], cols[:, 2:3], cols[:, 6:7],
                    ALU.subtract, ALU.mult))
            b.emit("vector", lambda e, xt=xt: e.tensor_scalar(
                xnb[:], xt[:], cols[:, 2:3], cols[:, 7:8], ALU.subtract, ALU.mult),
                [("eV", 1)])
            tile_done[i] = b.cnt["eV"]
            b.emit("vector", lambda e: e.tensor_tensor(xnb[:], xnb[:], lnw(), ALU.mult))
            b.emit("vector", lambda e: e.tensor_tensor(xbf[:], xnb[:], lnb(), ALU.add),
                   [("eV", 1)])
            b.wait("tensor", "eV", b.cnt["eV"])
            for which in range(2 if to_dram else 1):
                src = xbf if which == 0 else xbf2
                if which == 1 and i >= 2:
                    b.wait("vector", f"dX{i % 2}", stage_done[i - 2])
                for g in range(2):
                    par = g % 2
                    ps = psA if par == 0 else psB
                    b.wait("tensor", "eV", ps_free[par])
                    for k in range(4):
                        kk = g * 4 + k
                        b.emit("tensor", lambda e, ps=ps, k=k, kk=kk, src=src:
                               e.matmul(ps[:, k * 128:(k + 1) * 128],
                                        src[:, kk * 128:(kk + 1) * 128],
                                        identb_s[:], start=True, stop=True),
                               [("eT", 1)] if k == 3 else ())
                    b.wait("vector", "eT", b.cnt["eT"])
                    if which == 0:
                        for k in range(4):
                            kk = g * 4 + k
                            dstap = xn2sl(kk * Tn + i * 128, 128)
                            b.emit("vector", lambda e, ps=ps, k=k, dstap=dstap:
                                   e.tensor_copy(dstap,
                                                 ps[:, k * 128:(k + 1) * 128]),
                                   [("eV", 1)] if k == 3 else ())
                    else:
                        dstap = xst[i % 2][:, g * 512:(g + 1) * 512]
                        b.emit("vector", lambda e, ps=ps, dstap=dstap:
                               e.tensor_copy(dstap, ps[:, 0:512]), [("eV", 1)])
                    ps_free[par] = b.cnt["eV"]
                if which == 1:
                    b.wait("sync", "eV", b.cnt["eV"])
                    dsta = xn_scr.rearrange("p (k t) -> p k t", t=TM)[
                        :, :, i * 128:(i + 1) * 128]
                    srca = xst[i % 2][:].rearrange("p (k t) -> p k t", t=128)
                    b.emit("sync", lambda e, dsta=dsta, srca=srca:
                           e.dma_start(dsta, srca), [(f"dX{i % 2}", 16)])
                    stage_done[i] = b.cnt[f"dX{i % 2}"]
        b.wait("tensor", "eV", b.cnt["eV"])
        if to_dram:
            b.wait("sync", "dX0", b.cnt["dX0"])
            b.wait("sync", "dX1", b.cnt["dX1"])
        if debug and to_dram:
            b.wait("sync", "eV", b.cnt["eV"])
            b.wait("sync", "eS", b.cnt["eS"])
            for nm, src in [("d_xbf", xbf[:]), ("d_xbf2", xbf2[:]),
                            ("d_cols", cols[:]), ("d_xnb", xnb[:]),
                            ("d_lnw", lnw()), ("d_lnb", lnb()),
                            ("d_xt", xsb[(NT - 1) % 2][:]),
                            ("d_idb", identb_s[:]), ("d_pm", pmask_s[:])]:
                b.emit("sync", lambda e, nm=nm, src=src: e.dma_start(
                    dbg[nm][:], src), [("dO", 16)])
            b.dump_guard()
        b.free(f"lpsB{tag}", f"lpsA{tag}", f"lcol{tag}", f"lst{tag}1", f"lst{tag}0",
               f"lbf2{tag}", f"lbf{tag}", f"lsq{tag}", f"lxn{tag}",
               f"lx{tag}1", f"lx{tag}0")

    def inproj_phase(Tn, is_main, xn2sl, xcv, bcv, ccv, dtf, sp1, sp2, tag):
        NB = Tn // 512
        grpA = b.psum(f"ipA{tag}", [128, 2048])
        grpB = b.psum(f"ipB{tag}", [128, 2048])
        xtmp = [b.sbuf(f"ipx{tag}{i}", [128, Tn + 3], BF16) for i in range(2)]
        ztile = [b.sbuf(f"ipz{tag}{i}", [128, Tn], BF16)
                 for i in range(2)] if is_main else None
        ocs = [("xbc", oc) for oc in range(NXC)] + [("dt", 0)]
        if is_main:
            ocs += [("z", zc) for zc in range(NZC)]
        wfree = [b.cnt["eT"], b.cnt["eT"]]
        evA_v, evA_s = b.cnt["eV"], b.cnt["eS"]
        evB_s = b.cnt["eS"]
        xtmp_free = [b.cnt["eT"], b.cnt["eT"]]
        zst = {0: 0, 1: 0}
        for idx, (kind, oc) in enumerate(ocs):
            buf = idx % 2
            b.wait("gpsimd", "eT", wfree[buf])
            if kind == "xbc":
                src, ncol = wip[16 + oc].rearrange("p k o -> p (k o)"), 1024
            elif kind == "z":
                src, ncol = wip[oc].rearrange("p k o -> p (k o)"), 1024
            else:
                src, ncol = wipdt.rearrange("p k o -> p (k o)"), 256
            b.emit("gpsimd", lambda e, src=src, buf=buf, ncol=ncol: e.dma_start(
                WBUF[buf][:, :ncol], src), [(f"dG{buf}", 16)])
            if kind == "xbc" and (is_main or oc != 17):
                b.emit("gpsimd", lambda e, oc=oc, buf=buf: e.dma_start(
                    CDBUF[buf][:], dram["convdiag"][oc].rearrange("k o -> (k o)")
                    if False else dram["convdiag"][oc].rearrange("p k o -> p (k o)")),
                    [(f"dG{buf}", 16)])
            gm = b.cnt[f"dG{buf}"]
            b.wait("tensor", f"dG{buf}", gm)
            b.wait("tensor", "eV", evA_v)
            b.wait("tensor", "eS", evA_s)
            for i in range(8):
                for tb in range(NB):
                    if kind == "dt":
                        lhs = WBUF[buf][:, i * 32:(i + 1) * 32]
                        out = grpA[0:32, tb * 512:(tb + 1) * 512]
                    else:
                        lhs = WBUF[buf][:, i * 128:(i + 1) * 128]
                        out = grpA[:, tb * 512:(tb + 1) * 512]
                    rhs = xn2sl(i * Tn + tb * 512, 512)
                    last = (i == 7) and (tb == NB - 1)
                    b.emit("tensor", lambda e, out=out, lhs=lhs, rhs=rhs, i=i:
                           e.matmul(out, lhs, rhs, start=(i == 0), stop=(i == 7)),
                           [("eT", 1)] if last else ())
            wfree[buf] = b.cnt["eT"]
            tm = b.cnt["eT"]
            if kind == "z":
                zp = oc % 2
                b.wait("scalar", "eT", tm)
                b.wait("scalar", f"dZ{zp}", zst[zp])
                for tb in range(NB):
                    b.emit("scalar", lambda e, tb=tb, zp=zp: e.activation(
                        ztile[zp][:, tb * 512:(tb + 1) * 512],
                        grpA[:, tb * 512:(tb + 1) * 512], AF.Silu),
                        [("eS", 1)] if tb == NB - 1 else ())
                evA_s = b.cnt["eS"]
                b.wait("sync", "eS", evA_s)
                b.emit("sync", lambda e, oc=oc, zp=zp: e.dma_start(
                    zs_scr[:, oc * Tn: oc * Tn + Tn], ztile[zp][:, :Tn]),
                    [(f"dZ{zp}", 16)])
                zst[zp] = b.cnt[f"dZ{zp}"]
            elif kind == "dt":
                b.wait("vector", "eT", tm)
                b.emit("vector", lambda e: e.tensor_scalar(
                    sp1(0, Tn), grpA[0:32, :Tn], dtb_s[:], 30.0, ALU.add, ALU.min),
                    [("eV", 1)])
                b.wait("scalar", "eV", b.cnt["eV"])
                b.emit("scalar", lambda e: e.activation(sp2(0, Tn), sp1(0, Tn), AF.Exp))
                b.emit("scalar", lambda e: e.activation(sp2(0, Tn), sp2(0, Tn), AF.Ln, bias=1.0))
                b.wait("scalar", "eT", tm)
                b.emit("scalar", lambda e: e.activation(
                    sp1(0, Tn), grpA[0:32, :Tn], AF.Relu, bias=dtbm30_s[:]),
                    [("eS", 1)])
                evA_s = b.cnt["eS"]
                b.wait("vector", "eS", evA_s)
                b.emit("vector", lambda e: e.tensor_add(
                    dtf(0, Tn), sp2(0, Tn), sp1(0, Tn)), [("eV", 1)])
                evA_v = b.cnt["eV"]
            else:
                cur = xtmp[idx % 2]
                b.wait("vector", "eT", tm)
                b.wait("vector", "eT", xtmp_free[idx % 2])
                b.emit("vector", lambda e, cur=cur, oc=oc: e.tensor_copy(
                    cur[:, 0:3], xbct_s[:, oc * 3:(oc + 1) * 3]))
                for tb in range(NB):
                    if is_main:
                        b.emit("vector", lambda e, cur=cur, tb=tb: e.tensor_copy(
                            cur[:, 3 + tb * 512: 3 + (tb + 1) * 512],
                            grpA[:, tb * 512:(tb + 1) * 512]),
                            [("eV", 1)] if tb == NB - 1 else ())
                    else:
                        b.emit("vector", lambda e, cur=cur, tb=tb: e.tensor_tensor(
                            cur[:, 3 + tb * 512: 3 + (tb + 1) * 512],
                            grpA[:, tb * 512:(tb + 1) * 512],
                            pmask_s[:, tb * 512:(tb + 1) * 512], ALU.mult),
                            [("eV", 1)] if tb == NB - 1 else ())
                evA_v = b.cnt["eV"]
                if not is_main:
                    b.emit("vector", lambda e, oc=oc: e.tensor_tensor(
                        xbct_s[:, oc * 3:(oc + 1) * 3],
                        grpA[:, Tn - 3:Tn], pmask_s[:, Tn - 3:Tn], ALU.mult))
                if is_main or oc != 17:
                    b.wait("tensor", "eV", evA_v)
                    b.wait("tensor", "eS", evB_s)
                    for tb in range(NB):
                        for k in range(4):
                            last = (tb == NB - 1 and k == 3)
                            b.emit("tensor", lambda e, tb=tb, k=k, buf=buf, cur=cur:
                                   e.matmul(grpB[:, tb * 512:(tb + 1) * 512],
                                            CDBUF[buf][:, k * 128:(k + 1) * 128],
                                            cur[:, tb * 512 + k: tb * 512 + k + 512],
                                            start=(k == 0), stop=(k == 3)),
                                   [("eT", 1)] if last else ())
                    xtmp_free[idx % 2] = b.cnt["eT"]
                    wfree[buf] = b.cnt["eT"]
                    cm2 = b.cnt["eT"]
                    b.wait("scalar", "eT", cm2)
                    if oc < 16:
                        dst = lambda tb, oc=oc: xcv(oc * Tn + tb * 512, 512)
                    elif oc == 16:
                        dst = lambda tb: bcv(tb * 512, 512)
                    else:
                        dst = lambda tb: ccv(tb * 512, 512)
                    for tb in range(NB):
                        b.emit("scalar", lambda e, tb=tb, oc=oc, dst=dst: e.activation(
                            dst(tb), grpB[:, tb * 512:(tb + 1) * 512], AF.Silu,
                            bias=convb_s[:, oc:oc + 1]),
                            [("eS", 1)] if tb == NB - 1 else ())
                    evB_s = b.cnt["eS"]
        b.wait("tensor", "eV", b.cnt["eV"])
        b.wait("tensor", "eS", b.cnt["eS"])
        if is_main:
            b.wait("sync", "dZ0", b.cnt["dZ0"])
            b.wait("sync", "dZ1", b.cnt["dZ1"])
            b.free(f"ipz{tag}1", f"ipz{tag}0")
        b.free(f"ipx{tag}1", f"ipx{tag}0", f"ipB{tag}", f"ipA{tag}")

    def dt_phase(Tn, is_main, dtf, T4a, TCa, esf, es_bf, sp1, tag):
        NC = Tn // QC
        dpk = b.sbuf(f"dpk{tag}", [64, Tn], F32)
        sloc = dpk[0:32, :]
        dtA = dpk[32:64, :]
        srow = lambda: sp1(0, Tn)
        b.emit("vector", lambda e: e.tensor_scalar_mul(dtA[:], dtf(0, Tn), acol_s[:]))
        b.emit("vector", lambda e: e.tensor_tensor_scan(
            sloc[:], cmask_s[32:64, :Tn], dtA[:], 0.0, ALU.mult, ALU.add))
        send = sloc[:, QC - 1::QC]
        b.emit("vector", lambda e: e.tensor_copy(
            srow().rearrange("p (c q) -> p c q", q=QC),
            send.unsqueeze(2).broadcast_to([32, NC, QC])), [("eV", 1)])
        if is_main:
            b.emit("vector", lambda e: e.tensor_scalar_max(dtf(0, Tn), dtf(0, Tn), 1e-38))
            b.emit("vector", lambda e: e.tensor_copy(T4a(0, Tn, 96, 32), sloc[:]), [("eV", 1)])
            b.wait("scalar", "eV", b.cnt["eV"])
            b.emit("scalar", lambda e: e.activation(T4a(0, Tn, 0, 32), dtf(0, Tn), AF.Ln))
            b.emit("scalar", lambda e: e.activation(esf(0, Tn), sloc[:], AF.Exp))
            b.emit("scalar", lambda e: e.activation(T4a(0, Tn, 64, 32), srow(), AF.Exp),
                   [("eS", 1)])
            b.wait("vector", "eS", b.cnt["eS"])
            b.emit("vector", lambda e: e.tensor_copy(es_bf[:, :Tn], esf(0, Tn)))
            b.emit("vector", lambda e: e.tensor_tensor(
                T4a(0, Tn, 0, 32), T4a(0, Tn, 0, 32), sloc[:], ALU.subtract))
            b.emit("vector", lambda e: e.tensor_tensor(
                srow(), srow(), sloc[:], ALU.subtract), [("eV", 1)])
            b.wait("scalar", "eV", b.cnt["eV"])
            b.emit("scalar", lambda e: e.activation(srow(), srow(), AF.Exp),
                   [("eS", 1)])
            b.wait("vector", "eS", b.cnt["eS"])
            b.emit("vector", lambda e: e.tensor_tensor(
                T4a(0, Tn, 32, 32), srow(), dtf(0, Tn), ALU.mult), [("eV", 1)])
            nrow = 128
        else:
            b.wait("scalar", "eV", b.cnt["eV"])
            b.emit("scalar", lambda e: e.activation(T4a(0, Tn, 32, 32), srow(), AF.Exp),
                   [("eS", 1)])
            b.wait("vector", "eS", b.cnt["eS"])
            b.emit("vector", lambda e: e.tensor_tensor(
                srow(), srow(), sloc[:], ALU.subtract), [("eV", 1)])
            b.wait("scalar", "eV", b.cnt["eV"])
            b.emit("scalar", lambda e: e.activation(srow(), srow(), AF.Exp),
                   [("eS", 1)])
            b.wait("vector", "eS", b.cnt["eS"])
            b.emit("vector", lambda e: e.tensor_tensor(
                srow(), srow(), dtf(0, Tn), ALU.mult))
            b.emit("vector", lambda e: e.tensor_tensor(
                T4a(0, Tn, 0, 32), srow(), pmask_s[0:32, :Tn], ALU.mult), [("eV", 1)])
            nrow = 64
        vm = b.cnt["eV"]
        psT = b.psum(f"dtps{tag}", [128, 512])
        b.wait("tensor", "eV", vm)
        evm = b.cnt["eV"]
        for c in range(NC):
            slot = c % 4
            if slot == 0 and c > 0:
                b.wait("tensor", "eV", evm)
            b.emit("tensor", lambda e, c=c, slot=slot, nrow=nrow: e.transpose(
                psT[:, slot * 128: slot * 128 + nrow],
                T4a(c * QC, QC, 0, nrow), identf_s[0:nrow, 0:nrow]),
                [("eT", 1)] if (slot == 3 or c == NC - 1) else ())
            if slot == 3 or c == NC - 1:
                b.wait("vector", "eT", b.cnt["eT"])
                c0 = c - slot
                for s in range(slot + 1):
                    dst2 = TCa((c0 + s) * nrow, nrow)
                    b.emit("vector", lambda e, s=s, dst2=dst2, nrow=nrow:
                           e.tensor_copy(dst2, psT[:, s * 128: s * 128 + nrow]),
                           [("eV", 1)] if s == slot else ())
                evm = b.cnt["eV"]
        b.wait("tensor", "eV", evm)
        b.free(f"dtps{tag}", f"dpk{tag}")

    def ssd_phase(Tn, is_main, xcv, bcv, ccv, TCa, es_bf, zs, Hbf, Xtall, Xtw, tag,
                  zs_mark=0):
        NC = Tn // QC
        ps_S = b.psum(f"sS{tag}", [128, 2048])
        ps_T1 = b.psum(f"sT1{tag}", [128, 512])
        ps_T2 = b.psum(f"sT2{tag}", [128, 512])
        ps_M1 = b.psum(f"sM1{tag}", [128, 512])
        ps_M2 = b.psum(f"sM2{tag}", [128, 512])
        Bt = b.sbuf(f"Bt{tag}", [128, 128], BF16)
        Gtm = b.sbuf(f"Gtm{tag}", [128, 128], BF16)
        if is_main:
            c1b = b.sbuf(f"c1{tag}", [128, 1024], F32)
            Eb = b.sbuf(f"E{tag}", [128, 1024], BF16)
            Mb = [b.sbuf(f"Mt{tag}0", [128, 1024], BF16)] * 2
            Cb = [b.sbuf(f"Ce{tag}0", [128, 1024], BF16)] * 2
        ytmp = b.sbuf(f"yt{tag}", [128, 128], F32)
        t1_free = b.cnt["eV"]
        t2_free = b.cnt["eV"]
        re_free = b.cnt["eV"]
        mc_free = [b.cnt["eT"]] * 2
        yev_free = [b.cnt["eV"], b.cnt["eV"]]
        e_read = b.cnt["eS"]
        mc_read = b.cnt["eV"]
        for c in range(NC):
            b.wait("tensor", "eV", t2_free)
            b.emit("tensor", lambda e, c=c: e.matmul(
                ps_T2[:, 0:128], bcv(c * QC, QC), identb_s[:],
                start=True, stop=True), [("eT", 1)])
            b.wait("vector", "eT", b.cnt["eT"])
            b.emit("vector", lambda e: e.tensor_copy(Bt[:], ps_T2[:, 0:128]),
                   [("eV", 1)])
            t2_free = b.cnt["eV"]
            for q4 in range(4):
                par = q4 % 2
                ps = ps_T1 if par == 0 else ps_T2
                b.wait("tensor", "eV", t1_free if par == 0 else t2_free)
                for k in range(4):
                    cc = q4 * 4 + k
                    b.emit("tensor", lambda e, ps=ps, k=k, cc=cc, c=c: e.matmul(
                        ps[:, k * 128:(k + 1) * 128], xcv(cc * Tn + c * QC, QC),
                        identb_s[:], start=True, stop=True),
                        [("eT", 1)] if k == 3 else ())
                b.wait("vector", "eT", b.cnt["eT"])
                b.emit("vector", lambda e, ps=ps, q4=q4: e.tensor_copy(
                    Xtall[:, q4 * 512:(q4 + 1) * 512], ps[:, 0:512]), [("eV", 1)])
                if par == 0:
                    t1_free = b.cnt["eV"]
                else:
                    t2_free = b.cnt["eV"]
            wof = (c * 128 + 32) if is_main else (c * 64 + 0)
            b.emit("vector", lambda e, wof=wof: e.tensor_tensor(
                Xtw[:].rearrange("p (h q) -> p h q", q=HD),
                Xtall[:].rearrange("p (h q) -> p h q", q=HD),
                TCa(wof, 32).unsqueeze(2).broadcast_to([128, 32, HD]),
                ALU.mult), [("eV", 1)])
            if is_main:
                b.emit("vector", lambda e: e.tensor_copy(Hbf[:], H_s), [("eV", 1)])
            xwm = b.cnt["eV"]
            b.wait("tensor", "eV", xwm)
            b.wait("tensor", "eV", re_free)
            for h in range(NH):
                b.emit("tensor", lambda e, h=h: e.matmul(
                    ps_S[:, h * HD:(h + 1) * HD], Bt[:], Xtw[:, h * HD:(h + 1) * HD],
                    start=True, stop=True), [("eT", 1)] if h == NH - 1 else ())
            smm = b.cnt["eT"]
            lof = (c * 128 + 64) if is_main else (c * 64 + 32)
            b.wait("vector", "eT", smm)
            b.emit("vector", lambda e, lof=lof: e.tensor_tensor(
                H_s.rearrange("p (h q) -> p h q", q=HD),
                H_s.rearrange("p (h q) -> p h q", q=HD),
                TCa(lof, 32).unsqueeze(2).broadcast_to([128, 32, HD]),
                ALU.mult))
            b.emit("vector", lambda e: e.tensor_add(H_s, H_s, ps_S[:]),
                   [("eV", 1)])
            hup = b.cnt["eV"]
            b.wait("tensor", "eV", hup)
            if not is_main:
                continue
            b.emit("tensor", lambda e, c=c: e.matmul(
                ps_M1[:, 384:512], bcv(c * QC, QC), ccv(c * QC, QC),
                start=True, stop=True), [("eT", 1)])
            b.wait("vector", "eT", b.cnt["eT"])
            b.emit("vector", lambda e: e.tensor_tensor(
                Gtm[:], ps_M1[:, 384:512], cmtb_s[:], ALU.mult), [("eV", 1)])
            for g in range(NH // 8):
                pm = g % 2
                h0 = 8 * g
                psR = ps_S[:, 0:1024]
                psE = ps_S[:, 1024:2048]
                psY = ps_T1 if pm == 0 else ps_T2
                b.wait("tensor", "eV", re_free)
                for k in range(8):
                    scol = TCa(c * 128 + 96 + h0 + k, 1)
                    b.emit("tensor", lambda e, psR=psR, k=k, scol=scol: e.transpose(
                        psR[:, k * 128:(k + 1) * 128],
                        scol.broadcast_to([128, 128]), identf_s[:]))
                for k in range(8):
                    b.emit("tensor", lambda e, psE=psE, k=k, h=h0 + k, c=c: e.matmul(
                        psE[:, k * 128:(k + 1) * 128],
                        id32b_s[:, h:h + 1].broadcast_to([32, 128]),
                        es_bf[:, c * QC:(c + 1) * QC], start=True, stop=True),
                        [("eT", 1)] if k == 7 else ())
                rem = b.cnt["eT"]
                b.wait("vector", "eT", rem)
                b.wait("vector", "eS", e_read)
                b.emit("vector", lambda e, psR=psR, c=c, h0=h0: e.tensor_tensor(
                    c1b[:].rearrange("p (k t) -> p k t", t=128),
                    psR.rearrange("p (k t) -> p k t", t=128),
                    TCa(c * 128 + h0, 8).unsqueeze(2).broadcast_to([128, 8, 128]),
                    ALU.add))
                b.emit("vector", lambda e: e.tensor_scalar(
                    c1b[:], c1b[:], 1.0, 80.0, ALU.mult, ALU.min), [("eV", 1)])
                b.wait("scalar", "eV", b.cnt["eV"])
                b.wait("scalar", "eV", mc_read)
                b.emit("scalar", lambda e: e.activation(Eb[:], c1b[:], AF.Exp),
                       [("eS", 1)])
                e_read = b.cnt["eS"]
                b.wait("vector", "eS", e_read)
                b.wait("vector", "eT", mc_free[0])
                b.emit("vector", lambda e, pm=pm: e.tensor_tensor(
                    Mb[pm][:].rearrange("p (k t) -> p k t", t=128),
                    Eb[:].rearrange("p (k t) -> p k t", t=128),
                    Gtm[:].unsqueeze(1).broadcast_to([128, 8, 128]), ALU.mult))
                b.emit("vector", lambda e, pm=pm, psE=psE, c=c: e.tensor_tensor(
                    Cb[pm][:].rearrange("p (k t) -> p k t", t=128),
                    psE.rearrange("p (k t) -> p k t", t=128),
                    ccv(c * QC, QC).unsqueeze(1).broadcast_to([128, 8, 128]),
                    ALU.mult), [("eV", 1)])
                mc_read = b.cnt["eV"]
                re_free = b.cnt["eV"]
                b.wait("tensor", "eV", mc_read)
                b.wait("tensor", "eV", yev_free[pm])
                for s in range(4):
                    for hh in range(2):
                        k = 2 * s + hh
                        h = h0 + k
                        tp = (0, 0) if hh == 0 else (0, 64)
                        outsl = (psY[0:64, s * 128:(s + 1) * 128] if hh == 0
                                 else psY[64:128, s * 128:(s + 1) * 128])
                        b.emit("tensor", lambda e, outsl=outsl, h=h, pm=pm, k=k,
                               tp=tp: e.matmul(
                                   outsl, Xtall[:, h * HD:(h + 1) * HD],
                                   Mb[pm][:, k * 128:(k + 1) * 128],
                                   start=True, stop=False, tile_position=tp))
                        b.emit("tensor", lambda e, outsl=outsl, h=h, pm=pm, k=k,
                               tp=tp: e.matmul(
                                   outsl, Hbf[:, h * HD:(h + 1) * HD],
                                   Cb[pm][:, k * 128:(k + 1) * 128],
                                   start=False, stop=True, tile_position=tp),
                               [("eT", 1)] if (s == 3 and hh == 1) else ())
                mc_free[0] = b.cnt["eT"]
                if c == 0 and g == 0:
                    b.wait("vector", "dS", zs_mark)
                b.wait("vector", "eT", b.cnt["eT"])
                for s in range(4):
                    p = 4 * g + s
                    b.emit("vector", lambda e, p=p, c=c, psY=psY, s=s:
                           e.scalar_tensor_tensor(
                               ytmp[:], xcv(p * Tn + c * QC, QC),
                               dexp_s[:, p:p + 1],
                               psY[:, s * 128:(s + 1) * 128], ALU.mult, ALU.add))
                    b.emit("vector", lambda e, p=p, c=c: e.tensor_tensor(
                        zs(p * Tn + c * QC, QC), ytmp[:],
                        zs(p * Tn + c * QC, QC), ALU.mult),
                        [("eV", 1)] if s == 3 else ())
                yev_free[pm] = b.cnt["eV"]
            t1_free = max(t1_free, yev_free[0])
            t2_free = max(t2_free, yev_free[1])
        names = [f"yt{tag}"]
        if is_main:
            names += [f"Ce{tag}0", f"Mt{tag}0", f"E{tag}", f"c1{tag}"]
        names += [f"Gtm{tag}", f"Bt{tag}",
                  f"sM2{tag}", f"sM1{tag}", f"sT2{tag}", f"sT1{tag}", f"sS{tag}"]
        b.free(*names)

    def gating_phase(zs, TM):
        NB = TM // 512
        psG1 = b.psum("gms", [128, 2048])
        psG2 = b.psum("grn", [128, 2048])
        y2bf = [b.sbuf(f"y2bf{i}", [128, 512], BF16) for i in range(2)]
        rn = b.sbuf("rn", [1, TM], F32)
        rnbf = b.sbuf("rnbf", [1, TM], BF16)
        for tb in range(NB):
            tmark = b.cnt["eT"]
            for zc in range(NZC):
                par = zc % 2
                if zc >= 2:
                    b.wait("scalar", "eT", tmark + zc - 1)
                b.emit("scalar", lambda e, zc=zc, tb=tb, par=par: e.activation(
                    y2bf[par][:], zs(zc * TM + tb * 512, 512),
                    AF.Square), [("eS", 1)])
                b.wait("tensor", "eS", b.cnt["eS"])
                b.emit("tensor", lambda e, tb=tb, par=par, zc=zc: e.matmul(
                    psG1[0:1, tb * 512:(tb + 1) * 512], ones128b_s[:], y2bf[par][:],
                    start=(zc == 0), stop=(zc == NZC - 1)), [("eT", 1)])
        b.wait("vector", "eT", b.cnt["eT"])
        b.emit("vector", lambda e: e.tensor_scalar(
            rn[:], psG1[0:1, 0:TM], 1.0 / DIN, EPS, ALU.mult, ALU.add), [("eV", 1)])
        b.wait("vector", "eV", b.cnt["eV"])
        b.emit("vector", lambda e: e.reciprocal(rn[:], rn[:]), [("eV", 1)])
        b.wait("scalar", "eV", b.cnt["eV"])
        b.emit("scalar", lambda e: e.activation(rn[:], rn[:], AF.Sqrt), [("eS", 1)])
        b.wait("vector", "eS", b.cnt["eS"])
        b.emit("vector", lambda e: e.tensor_copy(rnbf[:], rn[:]), [("eV", 1)])
        b.wait("tensor", "eV", b.cnt["eV"])
        for tb in range(NB):
            b.emit("tensor", lambda e, tb=tb: e.matmul(
                psG2[:, tb * 512:(tb + 1) * 512], ones1b_s[:],
                rnbf[0:1, tb * 512:(tb + 1) * 512], start=True, stop=True),
                [("eT", 1)] if tb == NB - 1 else ())
        b.wait("vector", "eT", b.cnt["eT"])
        for zc in range(NZC):
            for tb in range(NB):
                b.emit("vector", lambda e, zc=zc, tb=tb: e.scalar_tensor_tensor(
                    zs(zc * TM + tb * 512, 512), zs(zc * TM + tb * 512, 512),
                    normw_s[:, zc:zc + 1], psG2[:, tb * 512:(tb + 1) * 512],
                    ALU.mult, ALU.mult),
                    [("eV", 1)] if (zc == NZC - 1 and tb == NB - 1) else ())
        b.wait("tensor", "eV", b.cnt["eV"])
        b.wait("scalar", "eV", b.cnt["eV"])
        b.free("rnbf", "rn", "y2bf1", "y2bf0", "grn", "gms")

    def layer(wt, n_oc, n_ic, rhs_fn, evac_fn, Tcols, tag):
        NB = max(Tcols // 512, 1)
        BW = min(Tcols, 512)
        grp = [b.psum(f"ly{tag}A", [128, 2048]), b.psum(f"ly{tag}B", [128, 2048])]
        wfree = [b.cnt["eT"]] * 2
        evv = [b.cnt["eV"]] * 2
        evs = [b.cnt["eS"]] * 2
        for oc in range(n_oc):
            buf = oc % 2
            gp = grp[buf]
            b.wait("gpsimd", "eT", wfree[buf])
            b.emit("gpsimd", lambda e, oc=oc, buf=buf: e.dma_start(
                WBUF[buf][:, : n_ic * 128], wt[oc].rearrange("p k o -> p (k o)")),
                [(f"dG{buf}", 16)])
            gm = b.cnt[f"dG{buf}"]
            b.wait("tensor", f"dG{buf}", gm)
            b.wait("tensor", "eV", evv[buf])
            b.wait("tensor", "eS", evs[buf])
            for i in range(n_ic):
                for tb in range(NB):
                    rhs = rhs_fn(i, tb)
                    b.emit("tensor", lambda e, gp=gp, i=i, tb=tb, buf=buf, rhs=rhs:
                           e.matmul(gp[:, tb * BW:(tb + 1) * BW],
                                    WBUF[buf][:, i * 128:(i + 1) * 128], rhs,
                                    start=(i == 0), stop=(i == n_ic - 1)),
                        [("eT", 1)] if (i == n_ic - 1 and tb == NB - 1) else ())
            wfree[buf] = b.cnt["eT"]
            tm = b.cnt["eT"]
            b.wait("vector", "eT", tm)
            b.wait("scalar", "eT", tm)
            for tb in range(NB):
                evac_fn(oc, tb, gp[:, tb * BW:(tb + 1) * BW])
            evv[buf] = b.cnt["eV"]
            evs[buf] = b.cnt["eS"]
        b.wait("tensor", "eV", b.cnt["eV"])
        b.wait("tensor", "eS", b.cnt["eS"])
        b.free(f"ly{tag}B", f"ly{tag}A")

    A.alloc("xn2", 8 * U_T, blk=4 * U_T)
    xn2sl = A.frz("xn2")
    ln_phase(TP, 0, xn2sl, False, tag="p")
    b.barrier()

    A.alloc("xcv", 16 * U_T, blk=U_T); A.alloc("bcv", U_T); A.alloc("ccv", U_T)
    A.alloc("dts", 2 * U_T); A.alloc("sp1", 2 * U_T); A.alloc("sp2", 2 * U_T)
    xcv = A.frz("xcv")
    bcv = A.frz("bcv")
    ccv = A.frz("ccv")
    dtf = lambda c0, w, _f=A.frz("dts", True): _f(c0, w)[0:32, :]
    sp1 = lambda c0, w, _f=A.frz("sp1", True): _f(c0, w)[0:32, :]
    sp2 = lambda c0, w, _f=A.frz("sp2", True): _f(c0, w)[0:32, :]
    inproj_phase(TP, False, xn2sl, xcv, bcv, ccv, dtf, sp1, sp2, tag="p")
    b.barrier()
    if debug:
        b.wait("sync", "eV", b.cnt["eV"])
        b.emit("sync", lambda e: e.dma_start(dbg["d_xbct2"][:], xbct_s[:]), [("dO", 16)])
        b.dump_guard()
    A.free("sp2")

    A.alloc("T4", 2 * U_T)
    A.alloc("TC", 2 * U_T)

    _T4f = A.frz("T4", True)

    def T4a(c0, w, r0=0, nr=32, _f=_T4f):
        return _f(c0, w)[r0:r0 + nr, :]
    TCa = A.frz("TC", True)
    dt_phase(TP, False, dtf, T4a, TCa, None, None, sp1, tag="p")
    b.barrier()

    A.alloc("Xtall", 2); A.alloc("Xtw", 2)
    Xtall_s = A.sl("Xtall", 0, 2048)
    Xtw_s = A.sl("Xtw", 0, 2048)
    ssd_phase(TP, False, xcv, bcv, None, TCa, None, None, None, Xtall_s, Xtw_s,
              tag="p")
    b.barrier()
    A.free("dts", "sp1")
    if debug:
        b.wait("sync", "eV", b.cnt["eV"])
        b.emit("sync", lambda e: e.dma_start(dbg["d_H0"][:], H_s), [("dO", 16)])
        b.emit("sync", lambda e: e.dma_start(dbg["d_xbct"][:], xbct_s[:]), [("dO", 16)])
        b.dump_guard()

    ln_phase(TM, TP, xn2sl, True, tag="m")
    b.barrier()
    A.free("lnw", "lnb")
    if debug:
        b.wait("sync", "eV", b.cnt["eV"])
        for g_ in range(2):
            b.emit("sync", lambda e, g_=g_: e.dma_start(
                dbg["d_xn2m"][:, g_ * 4 * TM:(g_ + 1) * 4 * TM],
                xn2sl(g_ * 4 * TM, 4 * TM)), [("dO", 16)])
        b.dump_guard()

    A.alloc("dts", 2 * U_T); A.alloc("sp1", 2 * U_T); A.alloc("sp2", 2 * U_T)
    dtf = lambda c0, w, _f=A.frz("dts", True): _f(c0, w)[0:32, :]
    sp1 = lambda c0, w, _f=A.frz("sp1", True): _f(c0, w)[0:32, :]
    sp2 = lambda c0, w, _f=A.frz("sp2", True): _f(c0, w)[0:32, :]
    inproj_phase(TM, True, xn2sl, xcv, bcv, ccv, dtf, sp1, sp2, tag="m")
    b.barrier()
    A.free("xn2")
    if debug:
        b.wait("sync", "eV", b.cnt["eV"])
        b.wait("sync", "eS", b.cnt["eS"])
        for cc in range(16):
            b.emit("sync", lambda e, cc=cc: e.dma_start(
                dbg["d_xcvx"][:, cc * TM:(cc + 1) * TM], xcv(cc * TM, TM)), [("dO", 16)])
        b.emit("sync", lambda e: e.dma_start(dbg["d_bcv"][:], bcv(0, TM)), [("dO", 16)])
        b.emit("sync", lambda e: e.dma_start(dbg["d_ccv"][:], ccv(0, TM)), [("dO", 16)])
        b.emit("sync", lambda e: e.dma_start(dbg["d_dt"][:], dtf(0, TM)), [("dO", 16)])
        b.dump_guard()

    A.alloc("es", 2 * U_T)
    esf = lambda c0, w, _f=A.frz("es", True): _f(c0, w)[0:32, :]
    esbf_s = b.sbuf("esbf_s", [32, TM], BF16)
    dt_phase(TM, True, dtf, T4a, TCa, esf, esbf_s, sp1, tag="m")
    b.barrier()
    if debug:
        b.wait("sync", "eV", b.cnt["eV"])
        b.emit("sync", lambda e: e.dma_start(dbg["d_sloc"][:], T4a(0, TM, 96, 32)),
               [("dO", 16)])
        b.dump_guard()
    A.free("es", "T4", "dts", "sp1", "sp2")

    A.alloc("zs", NZC * U_T, blk=U_T)
    zs = A.frz("zs")
    for zc in range(NZC):
        b.emit("sync", lambda e, zc=zc: e.dma_start(
            zs(zc * TM, TM), zs_scr[:, zc * TM:(zc + 1) * TM]), [("dS", 16)])
    zs_mark = b.cnt["dS"]

    Hbf_s = b.sbuf("Hbf_s", [128, 2048], BF16)
    ssd_phase(TM, True, xcv, bcv, ccv, TCa, esbf_s, zs, Hbf_s, Xtall_s, Xtw_s,
              tag="m", zs_mark=zs_mark)
    b.barrier()
    b.free("Hbf_s", "esbf_s")
    A.free("Xtw", "Xtall", "TC", "ccv", "bcv", "xcv", "H")
    if debug:
        b.wait("sync", "eV", b.cnt["eV"])
        for zc in range(NZC):
            b.emit("sync", lambda e, zc=zc: e.dma_start(
                dbg["d_yg"][:, zc * TM:(zc + 1) * TM], zs(zc * TM, TM)), [("dO", 16)])
        b.dump_guard()

    gating_phase(zs, TM)
    b.barrier()
    if debug:
        b.wait("sync", "eV", b.cnt["eV"])
        for zc in range(NZC):
            b.emit("sync", lambda e, zc=zc: e.dma_start(
                dbg["d_ynorm"][:, zc * TM:(zc + 1) * TM], zs(zc * TM, TM)), [("dO", 16)])
        b.dump_guard()

    A.alloc("xnc", 8 * U_T, blk=4 * U_T)
    xnc = A.frz("xnc")
    A.alloc("acc", 8 * U_T, blk=U_T)
    acc = A.frz("acc")
    for g_ in range(2):
        b.emit("sync", lambda e, g_=g_: e.dma_start(
            xnc(g_ * 4 * TM, 4 * TM), xn_scr[:, g_ * 4 * TM:(g_ + 1) * 4 * TM]),
            [("dS", 16)])
    b.wait("vector", "dS", b.cnt["dS"])

    def op_evac(oc, tb, ps):
        dsta = acc(oc * TM + tb * 512, 512)
        srcx = xnc(oc * TM + tb * 512, 512)
        b.emit("vector", lambda e, ps=ps, dsta=dsta, srcx=srcx: e.tensor_tensor(
            dsta, ps, srcx, ALU.add), [("eV", 1)])
    layer(wop, 8, 16, lambda i, tb: zs(i * TM + tb * 512, 512), op_evac, TM,
          tag="op")
    b.barrier()
    A.free("zs")
    if debug:
        b.wait("sync", "eV", b.cnt["eV"])
        for oc in range(8):
            b.emit("sync", lambda e, oc=oc: e.dma_start(
                dbg["d_acc"][:, oc * TM:(oc + 1) * TM], acc(oc * TM, TM)), [("dO", 16)])
        b.dump_guard()

    A.alloc("g", 16 * U_T, blk=max(TH // UN, 1))
    gsl = A.frz("g")
    silu_t = b.sbuf("silu_t", [128, 1024], BF16)
    hsl = None
    for half in range(2):
        toff = half * TH
        NBH = max(TH // 512, 1)
        BW = min(TH, 512)
        grpW = [b.psum(f"fw{half}A", [128, 2048]), b.psum(f"fw{half}B", [128, 2048])]
        wfree = [b.cnt["eT"]] * 2
        evv = b.cnt["eV"]
        evs = b.cnt["eS"]
        for oc in range(32):
            for which in range(2):
                buf = which
                gp = grpW[buf]
                wt = w1 if which == 0 else w2
                b.wait("gpsimd", "eT", wfree[buf])
                b.emit("gpsimd", lambda e, wt=wt, oc=oc, buf=buf: e.dma_start(
                    WBUF[buf][:, :1024], wt[oc].rearrange("p k o -> p (k o)")),
                    [(f"dG{buf}", 16)])
                gm = b.cnt[f"dG{buf}"]
                b.wait("tensor", f"dG{buf}", gm)
                if which == 0:
                    b.wait("tensor", "eS", evs)
                else:
                    b.wait("tensor", "eV", evv)
                for i in range(8):
                    for tb in range(NBH):
                        rhs = xnc(i * TM + toff + tb * BW, BW)
                        b.emit("tensor", lambda e, gp=gp, i=i, tb=tb, rhs=rhs, buf=buf:
                               e.matmul(gp[:, tb * BW:(tb + 1) * BW],
                                        WBUF[buf][:, i * 128:(i + 1) * 128], rhs,
                                        start=(i == 0), stop=(i == 7)),
                               [("eT", 1)] if (i == 7 and tb == NBH - 1) else ())
                wfree[buf] = b.cnt["eT"]
                tm = b.cnt["eT"]
                if which == 0:
                    b.wait("scalar", "eT", tm)
                    for tb in range(NBH):
                        b.emit("scalar", lambda e, gp=gp, oc=oc, tb=tb: e.activation(
                            silu_t[:, tb * BW:(tb + 1) * BW],
                            gp[:, tb * BW:(tb + 1) * BW], AF.Silu,
                            scale=w1s_s[:, oc:oc + 1]),
                            [("eS", 1)] if tb == NBH - 1 else ())
                    evs = b.cnt["eS"]
                else:
                    b.wait("vector", "eT", tm)
                    b.wait("vector", "eS", evs)
                    for tb in range(NBH):
                        b.emit("vector", lambda e, gp=gp, oc=oc, tb=tb:
                               e.scalar_tensor_tensor(
                                   gsl(oc * TH + tb * BW, BW),
                                   gp[:, tb * BW:(tb + 1) * BW],
                                   w2s_s[:, oc:oc + 1],
                                   silu_t[:, tb * BW:(tb + 1) * BW],
                                   ALU.mult, ALU.mult),
                               [("eV", 1)] if tb == NBH - 1 else ())
                    evv = b.cnt["eV"]
        b.wait("tensor", "eV", b.cnt["eV"])
        b.wait("tensor", "eS", b.cnt["eS"])
        b.free(f"fw{half}B", f"fw{half}A")
        b.barrier()
        A.free("xnc")
        A.alloc("h", 16 * U_T, blk=max(TH // UN, 1))
        hsl = A.frz("h")

        def h1_evac(oc, tb, ps, hsl=hsl):
            dsta = hsl(oc * TH + tb * 512, min(TH, 512))
            b.emit("scalar", lambda e, oc=oc, ps=ps, dsta=dsta: e.activation(
                dsta, ps, AF.Gelu, scale=h1s_s[:, oc:oc + 1]), [("eS", 1)])
        layer(h1, 32, 32, lambda i, tb: gsl(i * TH + tb * 512, min(TH, 512)),
              h1_evac, TH, tag=f"h1{half}")
        b.barrier()
        if debug and half == 0:
            b.wait("sync", "eS", b.cnt["eS"])
            for oc_ in range(32):
                b.emit("sync", lambda e, oc_=oc_: e.dma_start(
                    dbg["d_g"][:, oc_ * TH:(oc_ + 1) * TH], hsl(oc_ * TH, TH)),
                    [("dO", 16)])
            b.dump_guard()

        def h2_evac(oc, tb, ps):
            dsta = gsl(oc * TH + tb * 512, min(TH, 512))
            b.emit("scalar", lambda e, oc=oc, ps=ps, dsta=dsta: e.activation(
                dsta, ps, AF.Gelu, scale=h2s_s[:, oc:oc + 1]), [("eS", 1)])
        layer(h2, 32, 32,
              lambda i, tb, hsl=hsl: hsl(i * TH + tb * 512, min(TH, 512)),
              h2_evac, TH, tag=f"h2{half}")
        b.barrier()
        if debug and half == 0:
            b.wait("sync", "eS", b.cnt["eS"])
            for oc_ in range(32):
                b.emit("sync", lambda e, oc_=oc_: e.dma_start(
                    dbg["d_hh"][:, oc_ * TH:(oc_ + 1) * TH], gsl(oc_ * TH, TH)),
                    [("dO", 16)])
            b.dump_guard()

        def w3_evac(oc, tb, ps, toff=toff):
            dsta = acc(oc * TM + toff + tb * 512, min(TH, 512))
            b.emit("vector", lambda e, oc=oc, ps=ps, dsta=dsta:
                   e.scalar_tensor_tensor(dsta, ps, w3s_s[:, oc:oc + 1],
                                          dsta, ALU.mult, ALU.add), [("eV", 1)])
        if half == 0:
            A.free("h")
            A.alloc("xnc", 8 * U_T, blk=4 * U_T)
            xnc = A.frz("xnc")
            for g_ in range(2):
                b.emit("sync", lambda e, g_=g_: e.dma_start(
                    xnc(g_ * 4 * TM, 4 * TM),
                    xn_scr[:, g_ * 4 * TM:(g_ + 1) * 4 * TM]), [("dS", 16)])
        layer(w3, 8, 32, lambda i, tb: gsl(i * TH + tb * 512, min(TH, 512)),
              w3_evac, TH, tag=f"w3{half}")
        b.barrier()
        if half == 0:
            b.wait("tensor", "dS", b.cnt["dS"])
        else:
            A.free("h")
    b.free("silu_t")
    A.free("g")

    psF1 = b.psum("fin1", [128, 512])
    psF2 = b.psum("fin2", [128, 512])
    xr = [b.sbuf(f"xr{i}", [128, D], F32) for i in range(2)]
    osb = [b.sbuf(f"osb{i}", [128, D], F32) for i in range(2)]
    NT = TM // 128
    tile_v = [0] * NT
    st_marks = [0] * NT
    ld_marks = [0] * NT
    for i in range(NT):
        if i >= 2:
            b.wait("sync", "eV", tile_v[i - 2])
        b.emit("sync", lambda e, i=i: e.dma_start(
            xr[i % 2][:], x_in[TP + i * 128: TP + (i + 1) * 128, :]),
            [(f"dF{i % 2}", 16)])
        ld_marks[i] = b.cnt[f"dF{i % 2}"]
        if i >= 2:
            b.wait("vector", f"dO{i % 2}", st_marks[i - 2])
        if i > 0:
            b.wait("tensor", "eV", tile_v[i - 1])
        for g in range(2):
            ps = psF1 if g == 0 else psF2
            for k in range(4):
                kk = g * 4 + k
                b.emit("tensor", lambda e, ps=ps, k=k, kk=kk, i=i: e.matmul(
                    ps[:, k * 128:(k + 1) * 128],
                    acc(kk * TM + i * 128, 128), identb_s[:],
                    start=True, stop=True),
                    [("eT", 1)] if k == 3 else ())
            b.wait("vector", "eT", b.cnt["eT"])
            b.wait("vector", f"dF{i % 2}", ld_marks[i])
            b.emit("vector", lambda e, ps=ps, g=g, i=i: e.tensor_add(
                osb[i % 2][:, g * 512:(g + 1) * 512], ps[:],
                xr[i % 2][:, g * 512:(g + 1) * 512]), [("eV", 1)])
        tile_v[i] = b.cnt["eV"]
        b.wait("sync", "eV", tile_v[i])
        b.emit("sync", lambda e, i=i: e.dma_start(
            y_out[i * 128:(i + 1) * 128, :], osb[i % 2][:]), [(f"dO{i % 2}", 16)])
        st_marks[i] = b.cnt[f"dO{i % 2}"]
    b.wait("sync", "dO0", b.cnt["dO0"])
    b.wait("sync", "dO1", b.cnt["dO1"])
    if debug:
        b.wait("sync", "dO", b.cnt["dO"])

    with b.stack:
        with nc.Block() as block:
            @block.sync
            def _(e):
                for fn in b.plan["sync"]:
                    fn(e)

            @block.gpsimd
            def _(e):
                for fn in b.plan["gpsimd"]:
                    fn(e)

            @block.tensor
            def _(e):
                for fn in b.plan["tensor"]:
                    fn(e)

            @block.vector
            def _(e):
                for fn in b.plan["vector"]:
                    fn(e)

            @block.scalar
            def _(e):
                for fn in b.plan["scalar"]:
                    fn(e)
    return nc


def kernel(**inputs):
    TP = TM = 2048
    maps = host_prepare(inputs, TP, TM)
    nc = build_nc(TP, TM, debug=False)
    res = run_bass_kernel_spmd(nc, maps, core_ids=list(range(8)))
    out = np.empty((4, 4096, 1024), np.float32)
    for bb in range(4):
        for hf in range(2):
            out[bb, hf * 2048:(hf + 1) * 2048] = res.results[bb * 2 + hf]["y_out"]
    return out


# version 20 (final)
# speedup vs baseline: 1.0067x; 1.0067x over previous
import contextlib
import numpy as np
import ml_dtypes
import concourse.bass as bass
import concourse.mybir as mybir
from concourse.bass_utils import run_bass_kernel_spmd

F32 = mybir.dt.float32
BF16 = mybir.dt.bfloat16
AF = mybir.ActivationFunctionType
ALU = mybir.AluOpType
AX = mybir.AxisListType

D = 1024
DIN = 2048
NH = 32
HD = 64
QC = 128
NZC = 16
NXC = 18
EPS = 1e-5
UN = 1024


def _swz(w):
    O, I = w.shape
    return np.ascontiguousarray(
        w.reshape(O // 128, 128, I // 128, 128).transpose(0, 3, 2, 1))


def _cols(s):
    return np.ascontiguousarray(np.asarray(s).reshape(-1, 128).T)


def host_prepare(inp, TP, TM):
    f = lambda a: np.ascontiguousarray(np.asarray(a, dtype=np.float32))
    sh = {}
    sh["wip"] = _swz(f(inp["L0_attn_in_proj_w"])[:4352])
    wdt = f(inp["L0_attn_in_proj_w"])[4352:4384]
    sh["wipdt"] = np.ascontiguousarray(wdt.reshape(32, 8, 128).transpose(2, 1, 0))
    sh["wop"] = _swz(f(inp["L0_attn_out_proj_w"]))
    sh["w1"] = _swz(f(inp["L0_ffn_w1_w"]))
    sh["w2"] = _swz(f(inp["L0_ffn_w2_w"]))
    sh["h1"] = _swz(f(inp["L0_ffn_h1_w"]))
    sh["h2"] = _swz(f(inp["L0_ffn_h2_w"]))
    sh["w3"] = _swz(f(inp["L0_ffn_w3_w"]))
    sh["w1s"] = _cols(f(inp["L0_ffn_w1_s"]))
    sh["w2s"] = _cols(f(inp["L0_ffn_w2_s"]))
    sh["h1s"] = _cols(f(inp["L0_ffn_h1_s"]))
    sh["h2s"] = _cols(f(inp["L0_ffn_h2_s"]))
    sh["w3s"] = _cols(f(inp["L0_ffn_w3_s"]))
    sh["normw"] = _cols(f(inp["L0_attn_norm_w"]))
    sh["dtb"] = f(inp["L0_attn_dt_bias"]).reshape(32, 1)
    sh["alog"] = f(inp["L0_attn_A_log"]).reshape(32, 1)
    sh["dexp"] = _cols(np.repeat(f(inp["L0_attn_D"]), HD))
    sh["convb"] = _cols(f(inp["L0_attn_conv_b"]))
    cw = f(inp["L0_attn_conv_w"])
    cd = np.zeros((NXC, 128, 4, 128), np.float32)
    idx = np.arange(128)
    for cc in range(NXC):
        for k in range(4):
            cd[cc, idx, k, idx] = cw[cc * 128 + idx, k]
    sh["convdiag"] = cd
    sh["lnw_b"] = np.ascontiguousarray(
        np.broadcast_to(f(inp["L0_attn_ln_w"])[None, :], (128, D)))
    sh["lnb_b"] = np.ascontiguousarray(
        np.broadcast_to(f(inp["L0_attn_ln_b"])[None, :], (128, D)))
    sh["identf"] = np.eye(128, dtype=np.float32)
    sh["id32"] = np.eye(32, dtype=np.float32)
    sh["ones1"] = np.ones((1, 128), np.float32)
    sh["ones128"] = np.ones((128, 1), np.float32)
    jj, ii = np.meshgrid(np.arange(128), np.arange(128), indexing="ij")
    sh["cmt"] = (ii >= jj).astype(np.float32)
    cm = np.ones((32, max(TP, TM)), np.float32)
    cm[:, ::QC] = 0.0
    sh["cmask"] = cm

    x = f(inp["L0_attn_x"])
    maps = []
    for bb in range(x.shape[0]):
        for hf in range(2):
            m = dict(sh)
            if hf == 0:
                xin = np.concatenate(
                    [np.zeros((TP, D), np.float32), x[bb, :TM]], axis=0)
                pm = np.zeros((128, TP), np.float32)
            else:
                xin = x[bb, :TP + TM]
                pm = np.ones((128, TP), np.float32)
            m["x_in"] = np.ascontiguousarray(xin)
            m["pmask"] = pm
            maps.append(m)
    return maps


class Bld:
    def __init__(self, nc):
        self.nc = nc
        self.plan = {k: [] for k in ("sync", "gpsimd", "tensor", "vector", "scalar")}
        self.sem = {}
        self.cnt = {}
        self.stack = contextlib.ExitStack()
        self._cms = {}
        self._order = []

    def new_sem(self, name):
        self.sem[name] = self.stack.enter_context(self.nc.semaphore(name))
        self.cnt[name] = 0

    def emit(self, eng, fn, incs=()):
        incs2 = [(self.sem[s], n) for s, n in incs]
        for s, n in incs:
            self.cnt[s] += n

        def run(e, fn=fn, incs2=incs2):
            ins = fn(e)
            for h, n in incs2:
                ins = ins.then_inc(h, n)
        self.plan[eng].append(run)

    def wait(self, eng, s, val):
        h = self.sem[s]
        self.plan[eng].append(lambda e, h=h, v=val: e.wait_ge(h, v))

    def barrier(self):
        esem = {"tensor": "eT", "vector": "eV", "scalar": "eS",
                "sync": "qS", "gpsimd": "qG"}
        engs = list(esem)
        for e in ("tensor", "vector", "scalar"):
            self.emit(e, lambda x: x.drain(), [(esem[e], 1)])
        for e in ("sync", "gpsimd"):
            self.emit(e, lambda x: x.nop(), [(esem[e], 1)])
        marks = {e: self.cnt[esem[e]] for e in engs}
        for e in engs:
            if e == "gpsimd":
                continue
            for o in engs:
                if o != e:
                    self.wait(e, esem[o], marks[o])

    def sbuf(self, name, shape, dtype):
        cm = self.nc.sbuf_tensor(name, shape, dtype)
        h = cm.__enter__()
        self._cms[name] = cm
        self._order.append(name)
        return h

    def psum(self, name, shape, dtype=F32):
        cm = self.nc.psum_tensor(name, shape, dtype)
        h = cm.__enter__()
        self._cms[name] = cm
        self._order.append(name)
        return h

    def free(self, *names):
        for n in sorted(names, key=lambda n: -self._order.index(n)):
            self._cms.pop(n).__exit__(None, None, None)

    def dump_guard(self):
        for eng in ("vector", "scalar", "tensor"):
            self.wait(eng, "dO", self.cnt["dO"])


class Arena:

    def __init__(self, bld, n_units):
        self.t = bld.sbuf("ARENA", [128, n_units * UN], BF16)
        self.free_units = [(0, n_units)]
        self.alloc_map = {}

    def _take(self, n):
        for i, (st, ln) in enumerate(self.free_units):
            if ln >= n:
                if ln == n:
                    self.free_units.pop(i)
                else:
                    self.free_units[i] = (st + n, ln - n)
                return st
        raise RuntimeError(f"arena OOM ({n} units); free={self.free_units}")

    def alloc(self, name, n, blk=None):
        blk = blk or n
        assert n % blk == 0
        starts = [self._take(blk) for _ in range(n // blk)]
        self.alloc_map[name] = (blk, starts)

    def free(self, *names):
        for name in names:
            blk, starts = self.alloc_map.pop(name)
            for st in starts:
                self.free_units.append((st, blk))
        self.free_units.sort()
        merged = []
        for st, ln in self.free_units:
            if merged and merged[-1][0] + merged[-1][1] == st:
                merged[-1] = (merged[-1][0], merged[-1][1] + ln)
            else:
                merged.append((st, ln))
        self.free_units = merged

    def sl(self, name, col0, w, f32=False):
        blk, starts = self.alloc_map[name]
        bw = blk * UN
        c0 = 2 * col0 if f32 else col0
        ww = 2 * w if f32 else w
        bi, off = divmod(c0, bw)
        assert off + ww <= bw, (name, col0, w)
        base = starts[bi] * UN + off
        ap = self.t[:, base: base + ww]
        return ap.bitcast(F32) if f32 else ap

    def nblk(self, name):
        blk, starts = self.alloc_map[name]
        return len(starts), blk * UN

    def frz(self, name, f32=False):
        blk, starts = self.alloc_map[name]
        t = self.t

        def sl(col0, w, _blk=blk, _st=tuple(starts), _f32=f32, _t=t):
            bw = _blk * UN
            c0 = 2 * col0 if _f32 else col0
            ww = 2 * w if _f32 else w
            bi, off = divmod(c0, bw)
            assert off + ww <= bw
            base = _st[bi] * UN + off
            ap = _t[:, base: base + ww]
            return ap.bitcast(F32) if _f32 else ap
        return sl


def build_nc(TP, TM, debug=False, rc=False):
    assert TP == TM
    TH = TM // 2
    U_T = max(TM // UN, 1)
    nc = bass.Bass("TRN2", target_bir_lowering=False, detect_race_conditions=rc)
    dram = {}

    def din(name, shape):
        dram[name] = nc.dram_tensor(name, list(shape), F32, kind="ExternalInput").ap()
        return dram[name]

    x_in = din("x_in", (TP + TM, D))
    din("pmask", (128, TP))
    din("cmask", (32, max(TP, TM)))
    wip = din("wip", (34, 128, 8, 128))
    wipdt = din("wipdt", (128, 8, 32))
    wop = din("wop", (8, 128, 16, 128))
    w1 = din("w1", (32, 128, 8, 128)); w2 = din("w2", (32, 128, 8, 128))
    h1 = din("h1", (32, 128, 32, 128)); h2 = din("h2", (32, 128, 32, 128))
    w3 = din("w3", (8, 128, 32, 128))
    for nm, shp in [("w1s", (128, 32)), ("w2s", (128, 32)), ("h1s", (128, 32)),
                    ("h2s", (128, 32)), ("w3s", (128, 8)), ("normw", (128, 16)),
                    ("dtb", (32, 1)), ("alog", (32, 1)), ("dexp", (128, 16)),
                    ("convb", (128, 18)), ("convdiag", (NXC, 128, 4, 128)),
                    ("lnw_b", (128, D)), ("lnb_b", (128, D)),
                    ("identf", (128, 128)), ("id32", (32, 32)),
                    ("ones1", (1, 128)), ("ones128", (128, 1)),
                    ("cmt", (128, 128))]:
        din(nm, shp)

    y_out = nc.dram_tensor("y_out", [TM, D], F32, kind="ExternalOutput").ap()
    xn_scr = nc.dram_tensor("xn_scr", [128, 8 * TM], BF16).ap()
    zs_scr = nc.dram_tensor("zs_scr", [128, NZC * TM], BF16).ap()

    dbg = {}
    if debug:
        def dout(name, shape, dtype):
            dbg[name] = nc.dram_tensor(name, list(shape), dtype,
                                       kind="ExternalOutput").ap()
        dout("d_xn2m", (128, 8 * TM), BF16)
        dout("d_zs0", (128, NZC * TM), BF16)
        dout("d_xcvx", (128, 16 * TM), BF16)
        dout("d_bcv", (128, TM), BF16)
        dout("d_ccv", (128, TM), BF16)
        dout("d_dt", (32, TM), F32)
        dout("d_sloc", (32, TM), F32)
        dout("d_H0", (128, 2048), F32)
        dout("d_xbct", (128, NXC * 3), BF16)
        dout("d_xbct2", (128, NXC * 3), BF16)

        dout("d_yg", (128, NZC * TM), BF16)
        dout("d_ynorm", (128, NZC * TM), BF16)
        dout("d_acc", (128, 8 * TM), BF16)
        dout("d_g", (128, 32 * TH), BF16)
        dout("d_hh", (128, 32 * TH), BF16)
        dout("d_xbf", (128, D), BF16)
        dout("d_xbf2", (128, D), BF16)
        dout("d_cols", (128, 8), F32)
        dout("d_xnb", (128, D), F32)
        dout("d_lnw", (128, D), F32)
        dout("d_lnb", (128, D), F32)
        dout("d_xt", (128, D), BF16)
        dout("d_idb", (128, 128), BF16)
        dout("d_pm", (128, TP), BF16)

    b = Bld(nc)
    for s in ("dS", "dG", "dO", "eT", "eV", "eS", "qS", "qG",
              "dG0", "dG1", "dX0", "dX1", "dZ0", "dZ1", "dF0", "dF1",
              "dO0", "dO1"):
        b.new_sem(s)

    identf_s = b.sbuf("identf_s", [128, 128], F32)
    identb_s = b.sbuf("identb_s", [128, 128], BF16)
    id32b_s = b.sbuf("id32b_s", [32, 32], BF16)
    ones1b_s = b.sbuf("ones1b_s", [1, 128], BF16)
    ones128b_s = b.sbuf("ones128b_s", [128, 1], BF16)
    cmtb_s = b.sbuf("cmtb_s", [128, 128], BF16)
    dexp_s = b.sbuf("dexp_s", [128, 16], F32)
    normw_s = b.sbuf("normw_s", [128, 16], F32)
    w1s_s = b.sbuf("w1s_s", [128, 32], F32); w2s_s = b.sbuf("w2s_s", [128, 32], F32)
    h1s_s = b.sbuf("h1s_s", [128, 32], F32); h2s_s = b.sbuf("h2s_s", [128, 32], F32)
    w3s_s = b.sbuf("w3s_s", [128, 8], F32)
    dtb_s = b.sbuf("dtb_s", [32, 1], F32)
    dtbm30_s = b.sbuf("dtbm30_s", [32, 1], F32)
    alog_s = b.sbuf("alog_s", [32, 1], F32)
    acol_s = b.sbuf("acol_s", [32, 1], F32)
    convb_s = b.sbuf("convb_s", [128, 18], F32)
    cmask_s = b.sbuf("cmask_s", [64, max(TP, TM)], BF16)
    xbct_s = b.sbuf("xbct_s", [128, NXC * 3], BF16)
    pmask_s = b.sbuf("pmask_s", [128, TP], BF16)
    WBUF = [b.sbuf("wbuf0", [128, 4096], BF16), b.sbuf("wbuf1", [128, 4096], BF16)]
    CDBUF = [b.sbuf("cdbuf0", [128, 512], BF16), b.sbuf("cdbuf1", [128, 512], BF16)]

    A = Arena(b, max(40 * U_T, 48))
    A.alloc("H", 4)
    H_s = A.sl("H", 0, 2048, f32=True)
    A.alloc("lnw", 2); A.alloc("lnb", 2)
    lnw_ap = A.sl("lnw", 0, D, f32=True)
    lnb_ap = A.sl("lnb", 0, D, f32=True)
    lnw = lambda _a=lnw_ap: _a
    lnb = lambda _a=lnb_ap: _a

    b.emit("gpsimd", lambda e: e.dma_start(cmask_s[32:64, :], dram["cmask"][:]),
           [("dG", 16)])
    for name, dst in [("identf", identf_s),
                      ("dexp", dexp_s), ("normw", normw_s), ("w1s", w1s_s),
                      ("w2s", w2s_s), ("h1s", h1s_s), ("h2s", h2s_s),
                      ("w3s", w3s_s), ("dtb", dtb_s), ("alog", alog_s),
                      ("convb", convb_s)]:
        b.emit("sync", lambda e, d=dst, s=dram[name]: e.dma_start(d[:], s[:]),
               [("dS", 16)])
    b.emit("sync", lambda e: e.dma_start(lnw(), dram["lnw_b"][:]), [("dS", 16)])
    b.emit("sync", lambda e: e.dma_start(lnb(), dram["lnb_b"][:]), [("dS", 16)])
    for name, dst in [("identf", identb_s), ("id32", id32b_s), ("ones1", ones1b_s),
                      ("ones128", ones128b_s), ("cmt", cmtb_s)]:
        b.emit("gpsimd", lambda e, d=dst, s=dram[name]: e.dma_start(d[:], s[:]),
               [("dG", 16)])
    b.emit("gpsimd", lambda e: e.dma_start(pmask_s[:], dram["pmask"][:]),
           [("dG", 16)])
    for eng in ("vector", "scalar", "tensor"):
        b.wait(eng, "dS", b.cnt["dS"])
        b.wait(eng, "dG", b.cnt["dG"])
    b.emit("scalar", lambda e: e.activation(acol_s[:], alog_s[:], AF.Exp), [("eS", 1)])
    b.wait("vector", "eS", b.cnt["eS"])
    b.emit("vector", lambda e: e.tensor_scalar_mul(acol_s[:], acol_s[:], -1.0))
    b.emit("vector", lambda e: e.tensor_scalar_add(dtbm30_s[:], dtb_s[:], -30.0))
    b.emit("vector", lambda e: e.memset(H_s, 0.0))
    b.emit("vector", lambda e: e.memset(xbct_s[:], 0.0), [("eV", 1)])

    def ln_phase(Tn, x_off, xn2sl, to_dram, tag):
        NT = Tn // 128
        xsb = [b.sbuf(f"lx{tag}{i}", [128, D], BF16) for i in range(2)]
        xnb = b.sbuf(f"lxn{tag}", [128, D], F32)
        xsq = b.sbuf(f"lsq{tag}", [128, D], BF16)
        xbf = b.sbuf(f"lbf{tag}", [128, D], BF16)
        xbf2 = b.sbuf(f"lbf2{tag}", [128, D], BF16)
        xst = [b.sbuf(f"lst{tag}{i}", [128, D], BF16) for i in range(2)]
        cols = b.sbuf(f"lcol{tag}", [128, 8], F32)
        psA = b.psum(f"lpsA{tag}", [128, 512])
        psB = b.psum(f"lpsB{tag}", [128, 512])
        tile_done = [0] * NT
        stage_done = [0] * NT
        ps_free = {0: b.cnt["eV"], 1: b.cnt["eV"]}

        xsqf = xsq[:].bitcast(F32)

        def vw():
            b.wait("vector", "eV", b.cnt["eV"])

        def stats(src):
            b.emit("vector", lambda e: e.reduce_sum(cols[:, 0:1], src[:], axis=AX.X))
            b.emit("vector", lambda e: e.tensor_tensor(
                xsqf, src[:, 0:512], src[:, 0:512], ALU.mult))
            b.emit("vector", lambda e: e.reduce_sum(cols[:, 6:7], xsqf, axis=AX.X))
            b.emit("vector", lambda e: e.tensor_tensor(
                xsqf, src[:, 512:1024], src[:, 512:1024], ALU.mult))
            b.emit("vector", lambda e: e.reduce_sum(cols[:, 7:8], xsqf, axis=AX.X),
                   [("eV", 1)])
            vw()
            b.emit("vector", lambda e: e.tensor_add(
                cols[:, 1:2], cols[:, 6:7], cols[:, 7:8]))
            b.emit("vector", lambda e: e.tensor_scalar_mul(
                cols[:, 2:3], cols[:, 0:1], 1.0 / D), [("eV", 1)])
            vw()
            b.emit("vector", lambda e: e.tensor_scalar_mul(cols[:, 3:4], cols[:, 1:2], 1.0 / D))
            b.emit("vector", lambda e: e.tensor_tensor(
                cols[:, 4:5], cols[:, 2:3], cols[:, 2:3], ALU.mult), [("eV", 1)])
            vw()
            b.emit("vector", lambda e: e.tensor_tensor(
                cols[:, 4:5], cols[:, 3:4], cols[:, 4:5], ALU.subtract), [("eV", 1)])
            vw()
            b.emit("vector", lambda e: e.tensor_scalar(
                cols[:, 4:5], cols[:, 4:5], 1.0, EPS, ALU.mult, ALU.add), [("eV", 1)])
            vw()
            b.emit("vector", lambda e: e.tensor_scalar(
                cols[:, 5:6], cols[:, 4:5], 1.0 + EPS, -EPS, ALU.mult, ALU.add),
                [("eV", 1)])
            vw()
            b.emit("vector", lambda e: e.reciprocal(cols[:, 6:8], cols[:, 4:6]), [("eV", 1)])
            b.wait("scalar", "eV", b.cnt["eV"])
            b.emit("scalar", lambda e: e.activation(
                cols[:, 6:8], cols[:, 6:8], AF.Sqrt), [("eS", 1)])
            b.wait("vector", "eS", b.cnt["eS"])

        b.wait("gpsimd", "eV", b.cnt["eV"])
        b.wait("gpsimd", "eS", b.cnt["eS"])
        b.wait("gpsimd", "eT", b.cnt["eT"])
        for i in range(NT):
            if i >= 2:
                b.wait("gpsimd", "eV", tile_done[i - 2])
            b.emit("gpsimd", lambda e, i=i: e.dma_start(
                xsb[i % 2][:], x_in[x_off + i * 128: x_off + (i + 1) * 128, :]),
                [(f"dG{i % 2}", 16)])
            ldm = b.cnt[f"dG{i % 2}"]
            xt = xsb[i % 2]
            b.wait("vector", f"dG{i % 2}", ldm)
            stats(xt)
            if to_dram:
                b.emit("vector", lambda e, xt=xt: e.tensor_scalar(
                    xbf2[:], xt[:], cols[:, 2:3], cols[:, 6:7],
                    ALU.subtract, ALU.mult))
            b.emit("vector", lambda e, xt=xt: e.tensor_scalar(
                xnb[:], xt[:], cols[:, 2:3], cols[:, 7:8], ALU.subtract, ALU.mult),
                [("eV", 1)])
            tile_done[i] = b.cnt["eV"]
            b.emit("vector", lambda e: e.tensor_tensor(xnb[:], xnb[:], lnw(), ALU.mult))
            b.emit("vector", lambda e: e.tensor_tensor(xbf[:], xnb[:], lnb(), ALU.add),
                   [("eV", 1)])
            b.wait("tensor", "eV", b.cnt["eV"])
            for which in range(2 if to_dram else 1):
                src = xbf if which == 0 else xbf2
                if which == 1 and i >= 2:
                    b.wait("vector", f"dX{i % 2}", stage_done[i - 2])
                for g in range(2):
                    par = g % 2
                    ps = psA if par == 0 else psB
                    b.wait("tensor", "eV", ps_free[par])
                    for k in range(4):
                        kk = g * 4 + k
                        b.emit("tensor", lambda e, ps=ps, k=k, kk=kk, src=src:
                               e.matmul(ps[:, k * 128:(k + 1) * 128],
                                        src[:, kk * 128:(kk + 1) * 128],
                                        identb_s[:], start=True, stop=True),
                               [("eT", 1)] if k == 3 else ())
                    b.wait("vector", "eT", b.cnt["eT"])
                    if which == 0:
                        for k in range(4):
                            kk = g * 4 + k
                            dstap = xn2sl(kk * Tn + i * 128, 128)
                            b.emit("vector", lambda e, ps=ps, k=k, dstap=dstap:
                                   e.tensor_copy(dstap,
                                                 ps[:, k * 128:(k + 1) * 128]),
                                   [("eV", 1)] if k == 3 else ())
                    else:
                        dstap = xst[i % 2][:, g * 512:(g + 1) * 512]
                        b.emit("vector", lambda e, ps=ps, dstap=dstap:
                               e.tensor_copy(dstap, ps[:, 0:512]), [("eV", 1)])
                    ps_free[par] = b.cnt["eV"]
                if which == 1:
                    b.wait("sync", "eV", b.cnt["eV"])
                    dsta = xn_scr.rearrange("p (k t) -> p k t", t=TM)[
                        :, :, i * 128:(i + 1) * 128]
                    srca = xst[i % 2][:].rearrange("p (k t) -> p k t", t=128)
                    b.emit("sync", lambda e, dsta=dsta, srca=srca:
                           e.dma_start(dsta, srca), [(f"dX{i % 2}", 16)])
                    stage_done[i] = b.cnt[f"dX{i % 2}"]
        b.wait("tensor", "eV", b.cnt["eV"])
        if to_dram:
            b.wait("sync", "dX0", b.cnt["dX0"])
            b.wait("sync", "dX1", b.cnt["dX1"])
        if debug and to_dram:
            b.wait("sync", "eV", b.cnt["eV"])
            b.wait("sync", "eS", b.cnt["eS"])
            for nm, src in [("d_xbf", xbf[:]), ("d_xbf2", xbf2[:]),
                            ("d_cols", cols[:]), ("d_xnb", xnb[:]),
                            ("d_lnw", lnw()), ("d_lnb", lnb()),
                            ("d_xt", xsb[(NT - 1) % 2][:]),
                            ("d_idb", identb_s[:]), ("d_pm", pmask_s[:])]:
                b.emit("sync", lambda e, nm=nm, src=src: e.dma_start(
                    dbg[nm][:], src), [("dO", 16)])
            b.dump_guard()
        b.free(f"lpsB{tag}", f"lpsA{tag}", f"lcol{tag}", f"lst{tag}1", f"lst{tag}0",
               f"lbf2{tag}", f"lbf{tag}", f"lsq{tag}", f"lxn{tag}",
               f"lx{tag}1", f"lx{tag}0")

    def inproj_phase(Tn, is_main, xn2sl, xcv, bcv, ccv, dtf, sp1, sp2, tag):
        NB = Tn // 512
        grpA = b.psum(f"ipA{tag}", [128, 2048])
        grpB = b.psum(f"ipB{tag}", [128, 2048])
        xtmp = [b.sbuf(f"ipx{tag}{i}", [128, Tn + 3], BF16) for i in range(2)]
        ztile = [b.sbuf(f"ipz{tag}{i}", [128, Tn], BF16)
                 for i in range(2)] if is_main else None
        ocs = [("xbc", oc) for oc in range(NXC)] + [("dt", 0)]
        if is_main:
            ocs += [("z", zc) for zc in range(NZC)]
        wfree = [b.cnt["eT"], b.cnt["eT"]]
        evA_v, evA_s = b.cnt["eV"], b.cnt["eS"]
        evB_s = b.cnt["eS"]
        xtmp_free = [b.cnt["eT"], b.cnt["eT"]]
        zst = {0: 0, 1: 0}
        for idx, (kind, oc) in enumerate(ocs):
            buf = idx % 2
            b.wait("gpsimd", "eT", wfree[buf])
            if kind == "xbc":
                src, ncol = wip[16 + oc].rearrange("p k o -> p (k o)"), 1024
            elif kind == "z":
                src, ncol = wip[oc].rearrange("p k o -> p (k o)"), 1024
            else:
                src, ncol = wipdt.rearrange("p k o -> p (k o)"), 256
            b.emit("gpsimd", lambda e, src=src, buf=buf, ncol=ncol: e.dma_start(
                WBUF[buf][:, :ncol], src), [(f"dG{buf}", 16)])
            if kind == "xbc" and (is_main or oc != 17):
                b.emit("gpsimd", lambda e, oc=oc, buf=buf: e.dma_start(
                    CDBUF[buf][:], dram["convdiag"][oc].rearrange("k o -> (k o)")
                    if False else dram["convdiag"][oc].rearrange("p k o -> p (k o)")),
                    [(f"dG{buf}", 16)])
            gm = b.cnt[f"dG{buf}"]
            b.wait("tensor", f"dG{buf}", gm)
            b.wait("tensor", "eV", evA_v)
            b.wait("tensor", "eS", evA_s)
            for i in range(8):
                for tb in range(NB):
                    if kind == "dt":
                        lhs = WBUF[buf][:, i * 32:(i + 1) * 32]
                        out = grpA[0:32, tb * 512:(tb + 1) * 512]
                    else:
                        lhs = WBUF[buf][:, i * 128:(i + 1) * 128]
                        out = grpA[:, tb * 512:(tb + 1) * 512]
                    rhs = xn2sl(i * Tn + tb * 512, 512)
                    last = (i == 7) and (tb == NB - 1)
                    b.emit("tensor", lambda e, out=out, lhs=lhs, rhs=rhs, i=i:
                           e.matmul(out, lhs, rhs, start=(i == 0), stop=(i == 7)),
                           [("eT", 1)] if last else ())
            wfree[buf] = b.cnt["eT"]
            tm = b.cnt["eT"]
            if kind == "z":
                zp = oc % 2
                b.wait("scalar", "eT", tm)
                b.wait("scalar", f"dZ{zp}", zst[zp])
                for tb in range(NB):
                    b.emit("scalar", lambda e, tb=tb, zp=zp: e.activation(
                        ztile[zp][:, tb * 512:(tb + 1) * 512],
                        grpA[:, tb * 512:(tb + 1) * 512], AF.Silu),
                        [("eS", 1)] if tb == NB - 1 else ())
                evA_s = b.cnt["eS"]
                b.wait("sync", "eS", evA_s)
                b.emit("sync", lambda e, oc=oc, zp=zp: e.dma_start(
                    zs_scr[:, oc * Tn: oc * Tn + Tn], ztile[zp][:, :Tn]),
                    [(f"dZ{zp}", 16)])
                zst[zp] = b.cnt[f"dZ{zp}"]
            elif kind == "dt":
                b.wait("vector", "eT", tm)
                b.emit("vector", lambda e: e.tensor_scalar(
                    sp1(0, Tn), grpA[0:32, :Tn], dtb_s[:], 30.0, ALU.add, ALU.min),
                    [("eV", 1)])
                b.wait("scalar", "eV", b.cnt["eV"])
                b.emit("scalar", lambda e: e.activation(sp2(0, Tn), sp1(0, Tn), AF.Exp))
                b.emit("scalar", lambda e: e.activation(sp2(0, Tn), sp2(0, Tn), AF.Ln, bias=1.0))
                b.wait("scalar", "eT", tm)
                b.emit("scalar", lambda e: e.activation(
                    sp1(0, Tn), grpA[0:32, :Tn], AF.Relu, bias=dtbm30_s[:]),
                    [("eS", 1)])
                evA_s = b.cnt["eS"]
                b.wait("vector", "eS", evA_s)
                b.emit("vector", lambda e: e.tensor_add(
                    dtf(0, Tn), sp2(0, Tn), sp1(0, Tn)), [("eV", 1)])
                evA_v = b.cnt["eV"]
            else:
                cur = xtmp[idx % 2]
                b.wait("vector", "eT", tm)
                b.wait("vector", "eT", xtmp_free[idx % 2])
                b.emit("vector", lambda e, cur=cur, oc=oc: e.tensor_copy(
                    cur[:, 0:3], xbct_s[:, oc * 3:(oc + 1) * 3]))
                for tb in range(NB):
                    if is_main:
                        b.emit("vector", lambda e, cur=cur, tb=tb: e.tensor_copy(
                            cur[:, 3 + tb * 512: 3 + (tb + 1) * 512],
                            grpA[:, tb * 512:(tb + 1) * 512]),
                            [("eV", 1)] if tb == NB - 1 else ())
                    else:
                        b.emit("vector", lambda e, cur=cur, tb=tb: e.tensor_tensor(
                            cur[:, 3 + tb * 512: 3 + (tb + 1) * 512],
                            grpA[:, tb * 512:(tb + 1) * 512],
                            pmask_s[:, tb * 512:(tb + 1) * 512], ALU.mult),
                            [("eV", 1)] if tb == NB - 1 else ())
                evA_v = b.cnt["eV"]
                if not is_main:
                    b.emit("vector", lambda e, oc=oc: e.tensor_tensor(
                        xbct_s[:, oc * 3:(oc + 1) * 3],
                        grpA[:, Tn - 3:Tn], pmask_s[:, Tn - 3:Tn], ALU.mult))
                if is_main or oc != 17:
                    b.wait("tensor", "eV", evA_v)
                    b.wait("tensor", "eS", evB_s)
                    for tb in range(NB):
                        for k in range(4):
                            last = (tb == NB - 1 and k == 3)
                            b.emit("tensor", lambda e, tb=tb, k=k, buf=buf, cur=cur:
                                   e.matmul(grpB[:, tb * 512:(tb + 1) * 512],
                                            CDBUF[buf][:, k * 128:(k + 1) * 128],
                                            cur[:, tb * 512 + k: tb * 512 + k + 512],
                                            start=(k == 0), stop=(k == 3)),
                                   [("eT", 1)] if last else ())
                    xtmp_free[idx % 2] = b.cnt["eT"]
                    wfree[buf] = b.cnt["eT"]
                    cm2 = b.cnt["eT"]
                    b.wait("scalar", "eT", cm2)
                    if oc < 16:
                        dst = lambda tb, oc=oc: xcv(oc * Tn + tb * 512, 512)
                    elif oc == 16:
                        dst = lambda tb: bcv(tb * 512, 512)
                    else:
                        dst = lambda tb: ccv(tb * 512, 512)
                    for tb in range(NB):
                        b.emit("scalar", lambda e, tb=tb, oc=oc, dst=dst: e.activation(
                            dst(tb), grpB[:, tb * 512:(tb + 1) * 512], AF.Silu,
                            bias=convb_s[:, oc:oc + 1]),
                            [("eS", 1)] if tb == NB - 1 else ())
                    evB_s = b.cnt["eS"]
        b.wait("tensor", "eV", b.cnt["eV"])
        b.wait("tensor", "eS", b.cnt["eS"])
        if is_main:
            b.wait("sync", "dZ0", b.cnt["dZ0"])
            b.wait("sync", "dZ1", b.cnt["dZ1"])
            b.free(f"ipz{tag}1", f"ipz{tag}0")
        b.free(f"ipx{tag}1", f"ipx{tag}0", f"ipB{tag}", f"ipA{tag}")

    def dt_phase(Tn, is_main, dtf, T4a, TCa, esf, es_bf, sp1, tag):
        NC = Tn // QC
        dpk = b.sbuf(f"dpk{tag}", [64, Tn], F32)
        sloc = dpk[0:32, :]
        dtA = dpk[32:64, :]
        srow = lambda: sp1(0, Tn)
        b.emit("vector", lambda e: e.tensor_scalar_mul(dtA[:], dtf(0, Tn), acol_s[:]))
        b.emit("vector", lambda e: e.tensor_tensor_scan(
            sloc[:], cmask_s[32:64, :Tn], dtA[:], 0.0, ALU.mult, ALU.add))
        send = sloc[:, QC - 1::QC]
        b.emit("vector", lambda e: e.tensor_copy(
            srow().rearrange("p (c q) -> p c q", q=QC),
            send.unsqueeze(2).broadcast_to([32, NC, QC])), [("eV", 1)])
        if is_main:
            b.emit("vector", lambda e: e.tensor_scalar_max(dtf(0, Tn), dtf(0, Tn), 1e-38))
            b.emit("vector", lambda e: e.tensor_copy(T4a(0, Tn, 96, 32), sloc[:]), [("eV", 1)])
            b.wait("scalar", "eV", b.cnt["eV"])
            b.emit("scalar", lambda e: e.activation(T4a(0, Tn, 0, 32), dtf(0, Tn), AF.Ln))
            b.emit("scalar", lambda e: e.activation(esf(0, Tn), sloc[:], AF.Exp))
            b.emit("scalar", lambda e: e.activation(T4a(0, Tn, 64, 32), srow(), AF.Exp),
                   [("eS", 1)])
            b.wait("vector", "eS", b.cnt["eS"])
            b.emit("vector", lambda e: e.tensor_copy(es_bf[:, :Tn], esf(0, Tn)))
            b.emit("vector", lambda e: e.tensor_tensor(
                T4a(0, Tn, 0, 32), T4a(0, Tn, 0, 32), sloc[:], ALU.subtract))
            b.emit("vector", lambda e: e.tensor_tensor(
                srow(), srow(), sloc[:], ALU.subtract), [("eV", 1)])
            b.wait("scalar", "eV", b.cnt["eV"])
            b.emit("scalar", lambda e: e.activation(srow(), srow(), AF.Exp),
                   [("eS", 1)])
            b.wait("vector", "eS", b.cnt["eS"])
            b.emit("vector", lambda e: e.tensor_tensor(
                T4a(0, Tn, 32, 32), srow(), dtf(0, Tn), ALU.mult), [("eV", 1)])
            nrow = 128
        else:
            b.wait("scalar", "eV", b.cnt["eV"])
            b.emit("scalar", lambda e: e.activation(T4a(0, Tn, 32, 32), srow(), AF.Exp),
                   [("eS", 1)])
            b.wait("vector", "eS", b.cnt["eS"])
            b.emit("vector", lambda e: e.tensor_tensor(
                srow(), srow(), sloc[:], ALU.subtract), [("eV", 1)])
            b.wait("scalar", "eV", b.cnt["eV"])
            b.emit("scalar", lambda e: e.activation(srow(), srow(), AF.Exp),
                   [("eS", 1)])
            b.wait("vector", "eS", b.cnt["eS"])
            b.emit("vector", lambda e: e.tensor_tensor(
                srow(), srow(), dtf(0, Tn), ALU.mult))
            b.emit("vector", lambda e: e.tensor_tensor(
                T4a(0, Tn, 0, 32), srow(), pmask_s[0:32, :Tn], ALU.mult), [("eV", 1)])
            nrow = 64
        vm = b.cnt["eV"]
        psT = b.psum(f"dtps{tag}", [128, 512])
        b.wait("tensor", "eV", vm)
        evm = b.cnt["eV"]
        for c in range(NC):
            slot = c % 4
            if slot == 0 and c > 0:
                b.wait("tensor", "eV", evm)
            b.emit("tensor", lambda e, c=c, slot=slot, nrow=nrow: e.transpose(
                psT[:, slot * 128: slot * 128 + nrow],
                T4a(c * QC, QC, 0, nrow), identf_s[0:nrow, 0:nrow]),
                [("eT", 1)] if (slot == 3 or c == NC - 1) else ())
            if slot == 3 or c == NC - 1:
                b.wait("vector", "eT", b.cnt["eT"])
                c0 = c - slot
                for s in range(slot + 1):
                    dst2 = TCa((c0 + s) * nrow, nrow)
                    b.emit("vector", lambda e, s=s, dst2=dst2, nrow=nrow:
                           e.tensor_copy(dst2, psT[:, s * 128: s * 128 + nrow]),
                           [("eV", 1)] if s == slot else ())
                evm = b.cnt["eV"]
        b.wait("tensor", "eV", evm)
        b.free(f"dtps{tag}", f"dpk{tag}")

    def ssd_phase(Tn, is_main, xcv, bcv, ccv, TCa, es_bf, zs, Hbf, Xtall, Xtw, tag,
                  zs_mark=0):
        NC = Tn // QC
        ps_S = b.psum(f"sS{tag}", [128, 2048])
        ps_T1 = b.psum(f"sT1{tag}", [128, 512])
        ps_T2 = b.psum(f"sT2{tag}", [128, 512])
        ps_M1 = b.psum(f"sM1{tag}", [128, 512])
        ps_M2 = b.psum(f"sM2{tag}", [128, 512])
        Bt = b.sbuf(f"Bt{tag}", [128, 128], BF16)
        Gtm = b.sbuf(f"Gtm{tag}", [128, 128], BF16)
        if is_main:
            c1b = b.sbuf(f"c1{tag}", [128, 1024], F32)
            Eb = b.sbuf(f"E{tag}", [128, 1024], BF16)
            Mb = [b.sbuf(f"Mt{tag}0", [128, 1024], BF16)] * 2
            Cb = [b.sbuf(f"Ce{tag}0", [128, 1024], BF16)] * 2
        ytmp = b.sbuf(f"yt{tag}", [128, 128], F32)
        t1_free = b.cnt["eV"]
        t2_free = b.cnt["eV"]
        re_free = b.cnt["eV"]
        mc_free = [b.cnt["eT"]] * 2
        yev_free = [b.cnt["eV"], b.cnt["eV"]]
        e_read = b.cnt["eS"]
        mc_read = b.cnt["eV"]
        for c in range(NC):
            b.wait("tensor", "eV", t2_free)
            b.emit("tensor", lambda e, c=c: e.matmul(
                ps_T2[:, 0:128], bcv(c * QC, QC), identb_s[:],
                start=True, stop=True), [("eT", 1)])
            b.wait("vector", "eT", b.cnt["eT"])
            b.emit("vector", lambda e: e.tensor_copy(Bt[:], ps_T2[:, 0:128]),
                   [("eV", 1)])
            t2_free = b.cnt["eV"]
            for q4 in range(4):
                par = q4 % 2
                ps = ps_T1 if par == 0 else ps_T2
                b.wait("tensor", "eV", t1_free if par == 0 else t2_free)
                for k in range(4):
                    cc = q4 * 4 + k
                    b.emit("tensor", lambda e, ps=ps, k=k, cc=cc, c=c: e.matmul(
                        ps[:, k * 128:(k + 1) * 128], xcv(cc * Tn + c * QC, QC),
                        identb_s[:], start=True, stop=True),
                        [("eT", 1)] if k == 3 else ())
                b.wait("vector", "eT", b.cnt["eT"])
                b.emit("vector", lambda e, ps=ps, q4=q4: e.tensor_copy(
                    Xtall[:, q4 * 512:(q4 + 1) * 512], ps[:, 0:512]), [("eV", 1)])
                if par == 0:
                    t1_free = b.cnt["eV"]
                else:
                    t2_free = b.cnt["eV"]
            wof = (c * 128 + 32) if is_main else (c * 64 + 0)
            b.emit("vector", lambda e, wof=wof: e.tensor_tensor(
                Xtw[:].rearrange("p (h q) -> p h q", q=HD),
                Xtall[:].rearrange("p (h q) -> p h q", q=HD),
                TCa(wof, 32).unsqueeze(2).broadcast_to([128, 32, HD]),
                ALU.mult), [("eV", 1)])
            if is_main:
                b.emit("vector", lambda e: e.tensor_copy(Hbf[:], H_s), [("eV", 1)])
            xwm = b.cnt["eV"]
            b.wait("tensor", "eV", xwm)
            b.wait("tensor", "eV", re_free)
            for h in range(NH):
                b.emit("tensor", lambda e, h=h: e.matmul(
                    ps_S[:, h * HD:(h + 1) * HD], Bt[:], Xtw[:, h * HD:(h + 1) * HD],
                    start=True, stop=True), [("eT", 1)] if h == NH - 1 else ())
            smm = b.cnt["eT"]
            lof = (c * 128 + 64) if is_main else (c * 64 + 32)
            b.wait("vector", "eT", smm)
            b.emit("vector", lambda e, lof=lof: e.tensor_tensor(
                H_s.rearrange("p (h q) -> p h q", q=HD),
                H_s.rearrange("p (h q) -> p h q", q=HD),
                TCa(lof, 32).unsqueeze(2).broadcast_to([128, 32, HD]),
                ALU.mult))
            b.emit("vector", lambda e: e.tensor_add(H_s, H_s, ps_S[:]),
                   [("eV", 1)])
            hup = b.cnt["eV"]
            b.wait("tensor", "eV", hup)
            if not is_main:
                continue
            b.emit("tensor", lambda e, c=c: e.matmul(
                ps_M1[:, 384:512], bcv(c * QC, QC), ccv(c * QC, QC),
                start=True, stop=True), [("eT", 1)])
            b.wait("vector", "eT", b.cnt["eT"])
            b.emit("vector", lambda e: e.tensor_tensor(
                Gtm[:], ps_M1[:, 384:512], cmtb_s[:], ALU.mult), [("eV", 1)])
            for g in range(NH // 8):
                pm = g % 2
                h0 = 8 * g
                psR = ps_S[:, 0:1024]
                psE = ps_S[:, 1024:2048]
                psY = ps_T1 if pm == 0 else ps_T2
                b.wait("tensor", "eV", re_free)
                for k in range(8):
                    scol = TCa(c * 128 + 96 + h0 + k, 1)
                    b.emit("tensor", lambda e, psR=psR, k=k, scol=scol: e.transpose(
                        psR[:, k * 128:(k + 1) * 128],
                        scol.broadcast_to([128, 128]), identf_s[:]))
                for k in range(8):
                    b.emit("tensor", lambda e, psE=psE, k=k, h=h0 + k, c=c: e.matmul(
                        psE[:, k * 128:(k + 1) * 128],
                        id32b_s[:, h:h + 1].broadcast_to([32, 128]),
                        es_bf[:, c * QC:(c + 1) * QC], start=True, stop=True),
                        [("eT", 1)] if k == 7 else ())
                rem = b.cnt["eT"]
                b.wait("vector", "eT", rem)
                b.wait("vector", "eS", e_read)
                b.emit("vector", lambda e, psR=psR, c=c, h0=h0: e.tensor_tensor(
                    c1b[:].rearrange("p (k t) -> p k t", t=128),
                    psR.rearrange("p (k t) -> p k t", t=128),
                    TCa(c * 128 + h0, 8).unsqueeze(2).broadcast_to([128, 8, 128]),
                    ALU.add))
                b.emit("vector", lambda e: e.tensor_scalar(
                    c1b[:], c1b[:], 1.0, 80.0, ALU.mult, ALU.min), [("eV", 1)])
                b.wait("scalar", "eV", b.cnt["eV"])
                b.wait("scalar", "eV", mc_read)
                b.emit("scalar", lambda e: e.activation(Eb[:], c1b[:], AF.Exp),
                       [("eS", 1)])
                e_read = b.cnt["eS"]
                b.wait("vector", "eS", e_read)
                b.wait("vector", "eT", mc_free[0])
                b.emit("vector", lambda e, pm=pm: e.tensor_tensor(
                    Mb[pm][:].rearrange("p (k t) -> p k t", t=128),
                    Eb[:].rearrange("p (k t) -> p k t", t=128),
                    Gtm[:].unsqueeze(1).broadcast_to([128, 8, 128]), ALU.mult))
                b.emit("vector", lambda e, pm=pm, psE=psE, c=c: e.tensor_tensor(
                    Cb[pm][:].rearrange("p (k t) -> p k t", t=128),
                    psE.rearrange("p (k t) -> p k t", t=128),
                    ccv(c * QC, QC).unsqueeze(1).broadcast_to([128, 8, 128]),
                    ALU.mult), [("eV", 1)])
                mc_read = b.cnt["eV"]
                re_free = b.cnt["eV"]
                b.wait("tensor", "eV", mc_read)
                b.wait("tensor", "eV", yev_free[pm])
                for s in range(4):
                    for hh in range(2):
                        k = 2 * s + hh
                        h = h0 + k
                        tp = (0, 0) if hh == 0 else (0, 64)
                        outsl = (psY[0:64, s * 128:(s + 1) * 128] if hh == 0
                                 else psY[64:128, s * 128:(s + 1) * 128])
                        b.emit("tensor", lambda e, outsl=outsl, h=h, pm=pm, k=k,
                               tp=tp: e.matmul(
                                   outsl, Xtall[:, h * HD:(h + 1) * HD],
                                   Mb[pm][:, k * 128:(k + 1) * 128],
                                   start=True, stop=False, tile_position=tp))
                        b.emit("tensor", lambda e, outsl=outsl, h=h, pm=pm, k=k,
                               tp=tp: e.matmul(
                                   outsl, Hbf[:, h * HD:(h + 1) * HD],
                                   Cb[pm][:, k * 128:(k + 1) * 128],
                                   start=False, stop=True, tile_position=tp),
                               [("eT", 1)] if (s == 3 and hh == 1) else ())
                mc_free[0] = b.cnt["eT"]
                if c == 0 and g == 0:
                    b.wait("vector", "dS", zs_mark)
                b.wait("vector", "eT", b.cnt["eT"])
                for s in range(4):
                    p = 4 * g + s
                    b.emit("vector", lambda e, p=p, c=c, psY=psY, s=s:
                           e.scalar_tensor_tensor(
                               ytmp[:], xcv(p * Tn + c * QC, QC),
                               dexp_s[:, p:p + 1],
                               psY[:, s * 128:(s + 1) * 128], ALU.mult, ALU.add))
                    b.emit("vector", lambda e, p=p, c=c: e.tensor_tensor(
                        zs(p * Tn + c * QC, QC), ytmp[:],
                        zs(p * Tn + c * QC, QC), ALU.mult),
                        [("eV", 1)] if s == 3 else ())
                yev_free[pm] = b.cnt["eV"]
            t1_free = max(t1_free, yev_free[0])
            t2_free = max(t2_free, yev_free[1])
        names = [f"yt{tag}"]
        if is_main:
            names += [f"Ce{tag}0", f"Mt{tag}0", f"E{tag}", f"c1{tag}"]
        names += [f"Gtm{tag}", f"Bt{tag}",
                  f"sM2{tag}", f"sM1{tag}", f"sT2{tag}", f"sT1{tag}", f"sS{tag}"]
        b.free(*names)

    def gating_phase(zs, TM):
        NB = TM // 512
        psG1 = b.psum("gms", [128, 2048])
        psG2 = b.psum("grn", [128, 2048])
        y2bf = [b.sbuf(f"y2bf{i}", [128, 512], BF16) for i in range(2)]
        rn = b.sbuf("rn", [1, TM], F32)
        rnbf = b.sbuf("rnbf", [1, TM], BF16)
        for tb in range(NB):
            tmark = b.cnt["eT"]
            for zc in range(NZC):
                par = zc % 2
                if zc >= 2:
                    b.wait("scalar", "eT", tmark + zc - 1)
                b.emit("scalar", lambda e, zc=zc, tb=tb, par=par: e.activation(
                    y2bf[par][:], zs(zc * TM + tb * 512, 512),
                    AF.Square), [("eS", 1)])
                b.wait("tensor", "eS", b.cnt["eS"])
                b.emit("tensor", lambda e, tb=tb, par=par, zc=zc: e.matmul(
                    psG1[0:1, tb * 512:(tb + 1) * 512], ones128b_s[:], y2bf[par][:],
                    start=(zc == 0), stop=(zc == NZC - 1)), [("eT", 1)])
        b.wait("vector", "eT", b.cnt["eT"])
        b.emit("vector", lambda e: e.tensor_scalar(
            rn[:], psG1[0:1, 0:TM], 1.0 / DIN, EPS, ALU.mult, ALU.add), [("eV", 1)])
        b.wait("vector", "eV", b.cnt["eV"])
        b.emit("vector", lambda e: e.reciprocal(rn[:], rn[:]), [("eV", 1)])
        b.wait("scalar", "eV", b.cnt["eV"])
        b.emit("scalar", lambda e: e.activation(rn[:], rn[:], AF.Sqrt), [("eS", 1)])
        b.wait("vector", "eS", b.cnt["eS"])
        b.emit("vector", lambda e: e.tensor_copy(rnbf[:], rn[:]), [("eV", 1)])
        b.wait("tensor", "eV", b.cnt["eV"])
        for tb in range(NB):
            b.emit("tensor", lambda e, tb=tb: e.matmul(
                psG2[:, tb * 512:(tb + 1) * 512], ones1b_s[:],
                rnbf[0:1, tb * 512:(tb + 1) * 512], start=True, stop=True),
                [("eT", 1)] if tb == NB - 1 else ())
        b.wait("vector", "eT", b.cnt["eT"])
        for zc in range(NZC):
            for tb in range(NB):
                b.emit("vector", lambda e, zc=zc, tb=tb: e.scalar_tensor_tensor(
                    zs(zc * TM + tb * 512, 512), zs(zc * TM + tb * 512, 512),
                    normw_s[:, zc:zc + 1], psG2[:, tb * 512:(tb + 1) * 512],
                    ALU.mult, ALU.mult),
                    [("eV", 1)] if (zc == NZC - 1 and tb == NB - 1) else ())
        b.wait("tensor", "eV", b.cnt["eV"])
        b.wait("scalar", "eV", b.cnt["eV"])
        b.free("rnbf", "rn", "y2bf1", "y2bf0", "grn", "gms")

    def layer(wt, n_oc, n_ic, rhs_fn, evac_fn, Tcols, tag):
        NB = max(Tcols // 512, 1)
        BW = min(Tcols, 512)
        grp = [b.psum(f"ly{tag}A", [128, 2048]), b.psum(f"ly{tag}B", [128, 2048])]
        wfree = [b.cnt["eT"]] * 2
        evv = [b.cnt["eV"]] * 2
        evs = [b.cnt["eS"]] * 2
        for oc in range(n_oc):
            buf = oc % 2
            gp = grp[buf]
            b.wait("gpsimd", "eT", wfree[buf])
            b.emit("gpsimd", lambda e, oc=oc, buf=buf: e.dma_start(
                WBUF[buf][:, : n_ic * 128], wt[oc].rearrange("p k o -> p (k o)")),
                [(f"dG{buf}", 16)])
            gm = b.cnt[f"dG{buf}"]
            b.wait("tensor", f"dG{buf}", gm)
            b.wait("tensor", "eV", evv[buf])
            b.wait("tensor", "eS", evs[buf])
            for i in range(n_ic):
                for tb in range(NB):
                    rhs = rhs_fn(i, tb)
                    b.emit("tensor", lambda e, gp=gp, i=i, tb=tb, buf=buf, rhs=rhs:
                           e.matmul(gp[:, tb * BW:(tb + 1) * BW],
                                    WBUF[buf][:, i * 128:(i + 1) * 128], rhs,
                                    start=(i == 0), stop=(i == n_ic - 1)),
                        [("eT", 1)] if (i == n_ic - 1 and tb == NB - 1) else ())
            wfree[buf] = b.cnt["eT"]
            tm = b.cnt["eT"]
            b.wait("vector", "eT", tm)
            b.wait("scalar", "eT", tm)
            for tb in range(NB):
                evac_fn(oc, tb, gp[:, tb * BW:(tb + 1) * BW])
            evv[buf] = b.cnt["eV"]
            evs[buf] = b.cnt["eS"]
        b.wait("tensor", "eV", b.cnt["eV"])
        b.wait("tensor", "eS", b.cnt["eS"])
        b.free(f"ly{tag}B", f"ly{tag}A")

    A.alloc("xn2", 8 * U_T, blk=4 * U_T)
    xn2sl = A.frz("xn2")
    ln_phase(TP, 0, xn2sl, False, tag="p")
    b.barrier()

    A.alloc("xcv", 16 * U_T, blk=U_T); A.alloc("bcv", U_T); A.alloc("ccv", U_T)
    A.alloc("dts", 2 * U_T); A.alloc("sp1", 2 * U_T); A.alloc("sp2", 2 * U_T)
    xcv = A.frz("xcv")
    bcv = A.frz("bcv")
    ccv = A.frz("ccv")
    dtf = lambda c0, w, _f=A.frz("dts", True): _f(c0, w)[0:32, :]
    sp1 = lambda c0, w, _f=A.frz("sp1", True): _f(c0, w)[0:32, :]
    sp2 = lambda c0, w, _f=A.frz("sp2", True): _f(c0, w)[0:32, :]
    inproj_phase(TP, False, xn2sl, xcv, bcv, ccv, dtf, sp1, sp2, tag="p")
    b.barrier()
    if debug:
        b.wait("sync", "eV", b.cnt["eV"])
        b.emit("sync", lambda e: e.dma_start(dbg["d_xbct2"][:], xbct_s[:]), [("dO", 16)])
        b.dump_guard()
    A.free("sp2")

    A.alloc("T4", 2 * U_T)
    A.alloc("TC", 2 * U_T)

    _T4f = A.frz("T4", True)

    def T4a(c0, w, r0=0, nr=32, _f=_T4f):
        return _f(c0, w)[r0:r0 + nr, :]
    TCa = A.frz("TC", True)
    dt_phase(TP, False, dtf, T4a, TCa, None, None, sp1, tag="p")
    b.barrier()

    A.alloc("Xtall", 2); A.alloc("Xtw", 2)
    Xtall_s = A.sl("Xtall", 0, 2048)
    Xtw_s = A.sl("Xtw", 0, 2048)
    ssd_phase(TP, False, xcv, bcv, None, TCa, None, None, None, Xtall_s, Xtw_s,
              tag="p")
    b.barrier()
    A.free("dts", "sp1")
    if debug:
        b.wait("sync", "eV", b.cnt["eV"])
        b.emit("sync", lambda e: e.dma_start(dbg["d_H0"][:], H_s), [("dO", 16)])
        b.emit("sync", lambda e: e.dma_start(dbg["d_xbct"][:], xbct_s[:]), [("dO", 16)])
        b.dump_guard()

    ln_phase(TM, TP, xn2sl, True, tag="m")
    b.barrier()
    A.free("lnw", "lnb")
    if debug:
        b.wait("sync", "eV", b.cnt["eV"])
        for g_ in range(2):
            b.emit("sync", lambda e, g_=g_: e.dma_start(
                dbg["d_xn2m"][:, g_ * 4 * TM:(g_ + 1) * 4 * TM],
                xn2sl(g_ * 4 * TM, 4 * TM)), [("dO", 16)])
        b.dump_guard()

    A.alloc("dts", 2 * U_T); A.alloc("sp1", 2 * U_T); A.alloc("sp2", 2 * U_T)
    dtf = lambda c0, w, _f=A.frz("dts", True): _f(c0, w)[0:32, :]
    sp1 = lambda c0, w, _f=A.frz("sp1", True): _f(c0, w)[0:32, :]
    sp2 = lambda c0, w, _f=A.frz("sp2", True): _f(c0, w)[0:32, :]
    inproj_phase(TM, True, xn2sl, xcv, bcv, ccv, dtf, sp1, sp2, tag="m")
    b.barrier()
    A.free("xn2")
    if debug:
        b.wait("sync", "eV", b.cnt["eV"])
        b.wait("sync", "eS", b.cnt["eS"])
        for cc in range(16):
            b.emit("sync", lambda e, cc=cc: e.dma_start(
                dbg["d_xcvx"][:, cc * TM:(cc + 1) * TM], xcv(cc * TM, TM)), [("dO", 16)])
        b.emit("sync", lambda e: e.dma_start(dbg["d_bcv"][:], bcv(0, TM)), [("dO", 16)])
        b.emit("sync", lambda e: e.dma_start(dbg["d_ccv"][:], ccv(0, TM)), [("dO", 16)])
        b.emit("sync", lambda e: e.dma_start(dbg["d_dt"][:], dtf(0, TM)), [("dO", 16)])
        b.dump_guard()

    A.alloc("es", 2 * U_T)
    esf = lambda c0, w, _f=A.frz("es", True): _f(c0, w)[0:32, :]
    esbf_s = b.sbuf("esbf_s", [32, TM], BF16)
    dt_phase(TM, True, dtf, T4a, TCa, esf, esbf_s, sp1, tag="m")
    b.barrier()
    if debug:
        b.wait("sync", "eV", b.cnt["eV"])
        b.emit("sync", lambda e: e.dma_start(dbg["d_sloc"][:], T4a(0, TM, 96, 32)),
               [("dO", 16)])
        b.dump_guard()
    A.free("es", "T4", "dts", "sp1", "sp2")

    A.alloc("zs", NZC * U_T, blk=U_T)
    zs = A.frz("zs")
    for zc in range(NZC):
        b.emit("sync", lambda e, zc=zc: e.dma_start(
            zs(zc * TM, TM), zs_scr[:, zc * TM:(zc + 1) * TM]), [("dS", 16)])
    zs_mark = b.cnt["dS"]

    Hbf_s = b.sbuf("Hbf_s", [128, 2048], BF16)
    ssd_phase(TM, True, xcv, bcv, ccv, TCa, esbf_s, zs, Hbf_s, Xtall_s, Xtw_s,
              tag="m", zs_mark=zs_mark)
    b.barrier()
    b.free("Hbf_s", "esbf_s")
    A.free("Xtw", "Xtall", "TC", "ccv", "bcv", "xcv", "H")
    if debug:
        b.wait("sync", "eV", b.cnt["eV"])
        for zc in range(NZC):
            b.emit("sync", lambda e, zc=zc: e.dma_start(
                dbg["d_yg"][:, zc * TM:(zc + 1) * TM], zs(zc * TM, TM)), [("dO", 16)])
        b.dump_guard()

    A.alloc("xnc", 8 * U_T, blk=4 * U_T)
    xnc = A.frz("xnc")
    for g_ in range(2):
        b.emit("sync", lambda e, g_=g_: e.dma_start(
            xnc(g_ * 4 * TM, 4 * TM), xn_scr[:, g_ * 4 * TM:(g_ + 1) * 4 * TM]),
            [("dS", 16)])

    gating_phase(zs, TM)
    b.barrier()
    if debug:
        b.wait("sync", "eV", b.cnt["eV"])
        for zc in range(NZC):
            b.emit("sync", lambda e, zc=zc: e.dma_start(
                dbg["d_ynorm"][:, zc * TM:(zc + 1) * TM], zs(zc * TM, TM)), [("dO", 16)])
        b.dump_guard()

    A.alloc("acc", 8 * U_T, blk=U_T)
    acc = A.frz("acc")
    b.wait("vector", "dS", b.cnt["dS"])

    def op_evac(oc, tb, ps):
        dsta = acc(oc * TM + tb * 512, 512)
        srcx = xnc(oc * TM + tb * 512, 512)
        b.emit("vector", lambda e, ps=ps, dsta=dsta, srcx=srcx: e.tensor_tensor(
            dsta, ps, srcx, ALU.add), [("eV", 1)])
    layer(wop, 8, 16, lambda i, tb: zs(i * TM + tb * 512, 512), op_evac, TM,
          tag="op")
    b.barrier()
    A.free("zs")
    if debug:
        b.wait("sync", "eV", b.cnt["eV"])
        for oc in range(8):
            b.emit("sync", lambda e, oc=oc: e.dma_start(
                dbg["d_acc"][:, oc * TM:(oc + 1) * TM], acc(oc * TM, TM)), [("dO", 16)])
        b.dump_guard()

    A.alloc("g", 16 * U_T, blk=max(TH // UN, 1))
    gsl = A.frz("g")
    silu_t = b.sbuf("silu_t", [128, 1024], BF16)
    hsl = None
    for half in range(2):
        toff = half * TH
        NBH = max(TH // 512, 1)
        BW = min(TH, 512)
        grpW = [b.psum(f"fw{half}A", [128, 2048]), b.psum(f"fw{half}B", [128, 2048])]
        wfree = [b.cnt["eT"]] * 2
        evv = b.cnt["eV"]
        evs = b.cnt["eS"]
        for oc in range(32):
            for which in range(2):
                buf = which
                gp = grpW[buf]
                wt = w1 if which == 0 else w2
                b.wait("gpsimd", "eT", wfree[buf])
                b.emit("gpsimd", lambda e, wt=wt, oc=oc, buf=buf: e.dma_start(
                    WBUF[buf][:, :1024], wt[oc].rearrange("p k o -> p (k o)")),
                    [(f"dG{buf}", 16)])
                gm = b.cnt[f"dG{buf}"]
                b.wait("tensor", f"dG{buf}", gm)
                if which == 0:
                    b.wait("tensor", "eS", evs)
                else:
                    b.wait("tensor", "eV", evv)
                for i in range(8):
                    for tb in range(NBH):
                        rhs = xnc(i * TM + toff + tb * BW, BW)
                        b.emit("tensor", lambda e, gp=gp, i=i, tb=tb, rhs=rhs, buf=buf:
                               e.matmul(gp[:, tb * BW:(tb + 1) * BW],
                                        WBUF[buf][:, i * 128:(i + 1) * 128], rhs,
                                        start=(i == 0), stop=(i == 7)),
                               [("eT", 1)] if (i == 7 and tb == NBH - 1) else ())
                wfree[buf] = b.cnt["eT"]
                tm = b.cnt["eT"]
                if which == 0:
                    b.wait("scalar", "eT", tm)
                    for tb in range(NBH):
                        b.emit("scalar", lambda e, gp=gp, oc=oc, tb=tb: e.activation(
                            silu_t[:, tb * BW:(tb + 1) * BW],
                            gp[:, tb * BW:(tb + 1) * BW], AF.Silu,
                            scale=w1s_s[:, oc:oc + 1]),
                            [("eS", 1)] if tb == NBH - 1 else ())
                    evs = b.cnt["eS"]
                else:
                    b.wait("vector", "eT", tm)
                    b.wait("vector", "eS", evs)
                    for tb in range(NBH):
                        b.emit("vector", lambda e, gp=gp, oc=oc, tb=tb:
                               e.scalar_tensor_tensor(
                                   gsl(oc * TH + tb * BW, BW),
                                   gp[:, tb * BW:(tb + 1) * BW],
                                   w2s_s[:, oc:oc + 1],
                                   silu_t[:, tb * BW:(tb + 1) * BW],
                                   ALU.mult, ALU.mult),
                               [("eV", 1)] if tb == NBH - 1 else ())
                    evv = b.cnt["eV"]
        b.wait("tensor", "eV", b.cnt["eV"])
        b.wait("tensor", "eS", b.cnt["eS"])
        b.free(f"fw{half}B", f"fw{half}A")
        b.barrier()
        A.free("xnc")
        A.alloc("h", 16 * U_T, blk=max(TH // UN, 1))
        hsl = A.frz("h")

        def h1_evac(oc, tb, ps, hsl=hsl):
            dsta = hsl(oc * TH + tb * 512, min(TH, 512))
            b.emit("scalar", lambda e, oc=oc, ps=ps, dsta=dsta: e.activation(
                dsta, ps, AF.Gelu, scale=h1s_s[:, oc:oc + 1]), [("eS", 1)])
        layer(h1, 32, 32, lambda i, tb: gsl(i * TH + tb * 512, min(TH, 512)),
              h1_evac, TH, tag=f"h1{half}")
        b.barrier()
        if debug and half == 0:
            b.wait("sync", "eS", b.cnt["eS"])
            for oc_ in range(32):
                b.emit("sync", lambda e, oc_=oc_: e.dma_start(
                    dbg["d_g"][:, oc_ * TH:(oc_ + 1) * TH], hsl(oc_ * TH, TH)),
                    [("dO", 16)])
            b.dump_guard()

        def h2_evac(oc, tb, ps):
            dsta = gsl(oc * TH + tb * 512, min(TH, 512))
            b.emit("scalar", lambda e, oc=oc, ps=ps, dsta=dsta: e.activation(
                dsta, ps, AF.Gelu, scale=h2s_s[:, oc:oc + 1]), [("eS", 1)])
        layer(h2, 32, 32,
              lambda i, tb, hsl=hsl: hsl(i * TH + tb * 512, min(TH, 512)),
              h2_evac, TH, tag=f"h2{half}")
        b.barrier()
        if debug and half == 0:
            b.wait("sync", "eS", b.cnt["eS"])
            for oc_ in range(32):
                b.emit("sync", lambda e, oc_=oc_: e.dma_start(
                    dbg["d_hh"][:, oc_ * TH:(oc_ + 1) * TH], gsl(oc_ * TH, TH)),
                    [("dO", 16)])
            b.dump_guard()

        def w3_evac(oc, tb, ps, toff=toff):
            dsta = acc(oc * TM + toff + tb * 512, min(TH, 512))
            b.emit("vector", lambda e, oc=oc, ps=ps, dsta=dsta:
                   e.scalar_tensor_tensor(dsta, ps, w3s_s[:, oc:oc + 1],
                                          dsta, ALU.mult, ALU.add), [("eV", 1)])
        if half == 0:
            A.free("h")
            A.alloc("xnc", 8 * U_T, blk=4 * U_T)
            xnc = A.frz("xnc")
            for g_ in range(2):
                b.emit("sync", lambda e, g_=g_: e.dma_start(
                    xnc(g_ * 4 * TM, 4 * TM),
                    xn_scr[:, g_ * 4 * TM:(g_ + 1) * 4 * TM]), [("dS", 16)])
        layer(w3, 8, 32, lambda i, tb: gsl(i * TH + tb * 512, min(TH, 512)),
              w3_evac, TH, tag=f"w3{half}")
        b.barrier()
        if half == 0:
            b.wait("tensor", "dS", b.cnt["dS"])
        else:
            A.free("h")
    b.free("silu_t")
    A.free("g")

    psF1 = b.psum("fin1", [128, 512])
    psF2 = b.psum("fin2", [128, 512])
    xr = [b.sbuf(f"xr{i}", [128, D], F32) for i in range(2)]
    osb = [b.sbuf(f"osb{i}", [128, D], F32) for i in range(2)]
    NT = TM // 128
    tile_v = [0] * NT
    st_marks = [0] * NT
    ld_marks = [0] * NT
    for i in range(NT):
        if i >= 2:
            b.wait("sync", "eV", tile_v[i - 2])
        b.emit("sync", lambda e, i=i: e.dma_start(
            xr[i % 2][:], x_in[TP + i * 128: TP + (i + 1) * 128, :]),
            [(f"dF{i % 2}", 16)])
        ld_marks[i] = b.cnt[f"dF{i % 2}"]
        if i >= 2:
            b.wait("vector", f"dO{i % 2}", st_marks[i - 2])
        if i > 0:
            b.wait("tensor", "eV", tile_v[i - 1])
        for g in range(2):
            ps = psF1 if g == 0 else psF2
            for k in range(4):
                kk = g * 4 + k
                b.emit("tensor", lambda e, ps=ps, k=k, kk=kk, i=i: e.matmul(
                    ps[:, k * 128:(k + 1) * 128],
                    acc(kk * TM + i * 128, 128), identb_s[:],
                    start=True, stop=True),
                    [("eT", 1)] if k == 3 else ())
            b.wait("vector", "eT", b.cnt["eT"])
            b.wait("vector", f"dF{i % 2}", ld_marks[i])
            b.emit("vector", lambda e, ps=ps, g=g, i=i: e.tensor_add(
                osb[i % 2][:, g * 512:(g + 1) * 512], ps[:],
                xr[i % 2][:, g * 512:(g + 1) * 512]), [("eV", 1)])
        tile_v[i] = b.cnt["eV"]
        b.wait("sync", "eV", tile_v[i])
        b.emit("sync", lambda e, i=i: e.dma_start(
            y_out[i * 128:(i + 1) * 128, :], osb[i % 2][:]), [(f"dO{i % 2}", 16)])
        st_marks[i] = b.cnt[f"dO{i % 2}"]
    b.wait("sync", "dO0", b.cnt["dO0"])
    b.wait("sync", "dO1", b.cnt["dO1"])
    if debug:
        b.wait("sync", "dO", b.cnt["dO"])

    with b.stack:
        with nc.Block() as block:
            @block.sync
            def _(e):
                for fn in b.plan["sync"]:
                    fn(e)

            @block.gpsimd
            def _(e):
                for fn in b.plan["gpsimd"]:
                    fn(e)

            @block.tensor
            def _(e):
                for fn in b.plan["tensor"]:
                    fn(e)

            @block.vector
            def _(e):
                for fn in b.plan["vector"]:
                    fn(e)

            @block.scalar
            def _(e):
                for fn in b.plan["scalar"]:
                    fn(e)
    return nc


def kernel(**inputs):
    TP = TM = 2048
    maps = host_prepare(inputs, TP, TM)
    nc = build_nc(TP, TM, debug=False)
    res = run_bass_kernel_spmd(nc, maps, core_ids=list(range(8)))
    out = np.empty((4, 4096, 1024), np.float32)
    for bb in range(4):
        for hf in range(2):
            out[bb, hf * 2048:(hf + 1) * 2048] = res.results[bb * 2 + hf]["y_out"]
    return out
